# version 02
# baseline (speedup 1.0000x reference)
import math
import contextlib
import numpy as np
import ml_dtypes

import concourse.bass as bass
import concourse.mybir as mybir
from concourse.bass_utils import run_bass_kernel_spmd
from concourse.alu_op_type import AluOpType as OP

F32 = mybir.dt.float32
BF16 = mybir.dt.bfloat16
I32 = mybir.dt.int32
AF = mybir.ActivationFunctionType
AX = mybir.AxisListType.X

T = 4096
D = 1024
NCORES = 8
NEG = -1e9

ENGS = ("pe", "dve", "act", "pool", "sp")
QUEUES = ("q_sp", "q_pool", "q_act", "q_dve")
Q2ENG = {"q_sp": "sp", "q_pool": "pool", "q_act": "act", "q_dve": "dve"}
QMS = {"q_sp": "m_sp", "q_pool": "m_pool", "q_act": "m_act", "q_dve": "m_dve"}


class Emitter:

    def __init__(self, nc, serialize=False):
        self.nc = nc
        self.serialize = serialize
        self.es = contextlib.ExitStack()
        self.cnt = {s: 0 for s in ENGS + QUEUES}
        self.waited = {e: {} for e in ENGS}
        self.last_w = {}
        self.readers = {}
        self.pending = {s: [] for s in ENGS}
        self.thunks = {e: [] for e in ENGS}
        self.host_inputs = {}
        self.sems = {}

    def setup_sems(self):
        for s in ENGS + QUEUES + tuple(QMS.values()):
            self.sems[s] = self.es.enter_context(self.nc.semaphore(s))
        self.qflush = {q: [] for q in QUEUES}
        self.qmcnt = {q: 0 for q in QUEUES}

    def sbuf(self, name, shape, dtype):
        return self.es.enter_context(self.nc.sbuf_tensor(name, list(shape), dtype))

    def psum(self, name, shape, dtype):
        return self.es.enter_context(self.nc.psum_tensor(name, list(shape), dtype))

    def _resolve(self, dep, what, eng=None):
        sem, val = dep
        if val is None and sem == eng == "pe":
            return None
        assert val is not None, f"consumer of pending milestone on {sem} ({what}) from {eng}"
        return sem, val

    def _queue_deps_to_milestones(self, deps):
        out = []
        for s, v in deps:
            if s not in QUEUES:
                out.append((s, v))
                continue
            fl = self.qflush[s]
            hit = None
            for k, covered in fl:
                if covered >= v:
                    hit = k
                    break
            if hit is None:
                eng = Q2ENG[s]
                cov = self.cnt[s]
                self.qmcnt[s] += 1
                hit = self.qmcnt[s]
                fl.append((hit, cov))
                sems = self.sems
                qm = QMS[s]

                def flushthunk(e, _s=s, _cov=cov, _qm=qm):
                    e.wait_ge(sems[_s], _cov)
                    e.nop().then_inc(sems[_qm], 1)
                self.thunks[eng].append(flushthunk)
            out.append((QMS[s], hit))
        return out

    def _emit(self, eng, fn, reads, writes, inc, inc_by):
        assert eng in ENGS
        if not hasattr(self, "oplog"):
            self.oplog = {e: [] for e in ENGS}
        import sys as _sys
        fr = _sys._getframe(2)
        self.oplog[eng].append((f"{fr.f_code.co_filename.split('/')[-1]}:{fr.f_lineno}", list(reads), list(writes), inc))
        deps = []
        if self.serialize:
            for s in ENGS + QUEUES:
                if self.cnt[s] > 0:
                    deps.append((s, self.cnt[s]))
        for k in reads:
            w = self.last_w.get(k)
            if w is not None:
                deps.append(self._resolve(w, k, eng))
        for k in writes:
            w = self.last_w.get(k)
            if w is not None:
                deps.append(self._resolve(w, k, eng))
            for s, v in self.readers.get(k, {}).items():
                deps.append(self._resolve((s, v), k, eng))
        deps = [d for d in deps if d is not None]
        deps = self._queue_deps_to_milestones(deps)
        waits = []
        hw = self.waited[eng]
        for s, v in sorted(set(deps)):
            if s == eng and inc:
                pass
            if hw.get(s, 0) >= v:
                continue
            hw[s] = v
            waits.append((s, v))
        mysem = eng
        if inc:
            self.cnt[mysem] += inc_by
            val = self.cnt[mysem]
            for kind, k in self.pending[mysem]:
                if kind == "w":
                    if k in self.last_w and self.last_w[k][0] == mysem and self.last_w[k][1] is None:
                        self.last_w[k] = (mysem, val)
                else:
                    rd = self.readers.get(k)
                    if rd and mysem in rd and rd[mysem] is None:
                        rd[mysem] = val
            self.pending[mysem] = []
        else:
            val = None
        for k in writes:
            self.last_w[k] = (mysem, val)
            self.readers[k] = {}
            if val is None:
                self.pending[mysem].append(("w", k))
        for k in reads:
            self.readers.setdefault(k, {})[mysem] = val
            if val is None:
                self.pending[mysem].append(("r", k))

        sems = self.sems
        def thunk(e, _waits=waits, _fn=fn, _inc=inc, _ib=inc_by, _sem=mysem):
            for s, v in _waits:
                e.wait_ge(sems[s], v)
            inst = _fn(e)
            if _inc:
                inst.then_inc(sems[_sem], _ib)
        self.thunks[eng].append(thunk)

    def op(self, eng, fn, reads=(), writes=(), inc=True):
        self._emit(eng, fn, list(reads), list(writes), inc, 1)

    def wait_only(self, eng, sem_targets):
        sems = self.sems
        def thunk(e, _t=list(sem_targets)):
            for s, v in _t:
                e.wait_ge(sems[s], v)
        self.thunks[eng].append(thunk)

    def dma(self, queue, out, in_, reads=(), writes=(), no_waw=False, **kw):
        eng = Q2ENG[queue]
        assert eng in ENGS
        deps = []
        if self.serialize:
            for s in ENGS + QUEUES:
                if self.cnt[s] > 0:
                    deps.append((s, self.cnt[s]))
        for k in reads:
            w = self.last_w.get(k)
            if w is not None:
                deps.append(self._resolve(w, k, eng))
        for k in writes:
            if no_waw:
                continue
            w = self.last_w.get(k)
            if w is not None:
                deps.append(self._resolve(w, k, eng))
            for s, v in self.readers.get(k, {}).items():
                deps.append(self._resolve((s, v), k, eng))
        deps = [d for d in deps if d is not None]
        deps = self._queue_deps_to_milestones(deps)
        waits = []
        hw = self.waited[eng]
        for s, v in sorted(set(deps)):
            if hw.get(s, 0) >= v:
                continue
            hw[s] = v
            waits.append((s, v))
        self.cnt[queue] += 16
        val = self.cnt[queue]
        for k in writes:
            self.last_w[k] = (queue, val)
            self.readers[k] = {}
        for k in reads:
            self.readers.setdefault(k, {})[queue] = val
        sems = self.sems
        def thunk(e, _waits=waits, _o=out, _i=in_, _q=queue, _kw=kw):
            for s, v in _waits:
                e.wait_ge(sems[s], v)
            e.dma_start(out=_o, in_=_i, **_kw).then_inc(sems[_q], 16)
        self.thunks[eng].append(thunk)

    def finish(self):
        for q in QUEUES:
            if self.cnt[q] > 0:
                self.wait_only(Q2ENG[q], [(q, self.cnt[q])])
        nc = self.nc
        with self.es:
            with nc.Block() as block:
                ths = self.thunks

                @block.tensor
                def _(e):
                    for t in ths["pe"]:
                        t(e)

                @block.vector
                def _(e):
                    for t in ths["dve"]:
                        t(e)

                @block.scalar
                def _(e):
                    for t in ths["act"]:
                        t(e)

                @block.gpsimd
                def _(e):
                    for t in ths["pool"]:
                        t(e)

                @block.sync
                def _(e):
                    for t in ths["sp"]:
                        t(e)
        return nc


GRAN = 512


def keys_of(name, c0, c1):
    return [(name, g) for g in range(c0 // GRAN, (c1 + GRAN - 1) // GRAN)]


class Region:

    def __init__(self, tensor, name, c0, cols, gran=GRAN):
        self.t = tensor
        self.name = name
        self.c0 = c0
        self.cols = cols
        self.gran = gran

    def ap(self, p0=0, p1=128, c0=0, c1=None):
        if c1 is None:
            c1 = self.cols
        assert c1 <= self.cols
        return self.t[p0:p1, self.c0 + c0:self.c0 + c1]

    def keys(self, c0=0, c1=None):
        if c1 is None:
            c1 = self.cols
        g = self.gran
        a, b = self.c0 + c0, self.c0 + c1
        return [(self.name, i) for i in range(a // g, (b + g - 1) // g)]


class Arena:
    def __init__(self, em, name, dtype, cols, gran=GRAN):
        self.name = name
        self.t = em.sbuf(name, [128, cols], dtype)
        self.cols = cols
        self.off = 0
        self.gran = gran

    def alloc(self, cols):
        g = self.gran
        cols_al = (cols + g - 1) // g * g
        assert self.off + cols_al <= self.cols, \
            f"arena {self.name} overflow: {self.off}+{cols_al}>{self.cols}"
        r = Region(self.t, self.name, self.off, cols, g)
        self.off += cols_al
        return r

    def reset(self, off=0):
        self.off = off


def dram_keys(name, rows=0, r0=0, c0=0, c1=0):
    return [("D", name)]



def _rope_tables():
    d_R = 32
    inv_freq = 1.0 / 10000.0 ** (np.arange(0, d_R, 2, dtype=np.float32) / d_R)
    t = np.arange(T, dtype=np.float32)
    freqs = np.outer(t, inv_freq)
    return np.cos(freqs).T.copy(), np.sin(freqs).T.copy()


class Prog:
    def __init__(self, serialize=False):
        self.nc = bass.Bass("TRN2", target_bir_lowering=False)
        self.em = Emitter(self.nc, serialize=serialize)
        self.em.setup_sems()
        em = self.em
        self.dram_t = {}
        self.input_names = set()
        self.abf = Arena(em, "abf", BF16, 55 * 1024)
        self.af32 = Arena(em, "af32", F32, 12 * 1024)
        self.asm = Arena(em, "asm", F32, 5120, gran=16)
        self.ps = [em.psum(f"ps{i}", [128, 512], F32) for i in range(6)]
        self.pt = [em.psum(f"pt{i}", [128, 1024], BF16) for i in range(2)]
        self.consts = {}
        self._setup_consts()
        self.uid = 0

    def unique(self, prefix):
        self.uid += 1
        return f"{prefix}{self.uid}"

    dbg_names = set()

    def dram(self, name, shape, dtype, kind="Internal"):
        if name not in self.dram_t:
            if kind == "Internal" and name in Prog.dbg_names:
                kind = "ExternalOutput"
            self.dram_t[name] = self.nc.dram_tensor(name, list(shape), dtype, kind=kind).ap()
        return self.dram_t[name]

    def inp(self, name, shape, dtype):
        self.input_names.add(name)
        return self.dram(name, shape, dtype, kind="ExternalInput")

    def outp(self, name, shape, dtype):
        return self.dram(name, shape, dtype, kind="ExternalOutput")

    def _setup_consts(self):
        em = self.em
        idx = np.arange(128)
        cdefs = {
            "c_ident_bf": (np.eye(128, dtype=np.float32), BF16, [128, 128]),
            "c_ident_f": (np.eye(128, dtype=np.float32), F32, [128, 128]),
            "c_causal": (np.where(idx[:, None] >= idx[None, :], 0.0, NEG).astype(np.float32), F32, [128, 128]),
            "c_bdiag": (np.where(idx[:, None] // 8 == idx[None, :] // 8, 0.0, NEG).astype(np.float32), F32, [128, 128]),
            "c_ones_col_bf": (np.ones((128, 1), np.float32), BF16, [128, 1]),
            "c_ones_row_f": (np.ones((1, 128), np.float32), F32, [1, 128]),
            "c_ones_row_bf": (np.ones((1, 128), np.float32), BF16, [1, 128]),
        }
        cos, sin = _rope_tables()
        cdefs["c_cos"] = (cos, BF16, [16, T])
        cdefs["c_sin"] = (sin, BF16, [16, T])
        memset_consts = {"c_ones_col_bf", "c_ones_row_f", "c_ones_row_bf"}
        for name, (arr, dt, shape) in cdefs.items():
            sb = em.sbuf(name + "_sb", shape, dt)
            if name in memset_consts:
                em.op("pool", lambda e, sb=sb: e.memset(sb[:, :], 1.0),
                      reads=[], writes=[("const", name)])
            else:
                ap = self.inp(name, shape, dt)
                em.host_inputs[name] = (arr.astype(ml_dtypes.bfloat16)
                                        if dt == BF16 else arr)
                em.dma("q_sp", sb[:, :], ap, reads=[], writes=[("const", name)])
            self.consts[name] = sb

    def const(self, name):
        return self.consts[name][:, :]

    _sconst_cache = None

    def sconst(self, value):
        if self._sconst_cache is None:
            self._sconst_cache = {}
        if value not in self._sconst_cache:
            r = self.asm.alloc(1)
            self.em.op("pool", lambda e: e.memset(r.ap(0, 128, 0, 1), float(value)),
                       reads=[], writes=r.keys())
            self._sconst_cache[value] = r
        return self._sconst_cache[value].ap(0, 128, 0, 1), self._sconst_cache[value].keys()

    def load_fm(self, dram_name, K, cols=None, dtype=BF16, arena=None, c0=0, c1=None):
        cols = T if cols is None else cols
        """Load a feature-major DRAM tensor [K, cols] into a resident SBUF
        region laid out [128, (K/128)*w] (chunk-major), casting to bf16."""
        if c1 is None:
            c1 = cols
        w = c1 - c0
        a = arena or self.abf
        kc = K // 128
        r = a.alloc(kc * w)
        src = self.dram_t[dram_name]
        sv = src.rearrange("(c p) t -> p c t", p=128)[:, :, c0:c1]
        dv = r.ap().rearrange("p (c t) -> p c t", c=kc)
        q = "q_pool" if src.dtype != dtype else "q_sp"
        self.em.dma(q, dv, sv, reads=dram_keys(dram_name, K, 0, c0, c1), writes=r.keys())
        return r

    def chunk(self, region, kc, c0, c1, w=None, p0=0, p1=128):
        w = T if w is None else w
        return region.ap(p0, p1, kc * w + c0, kc * w + c1)

    def ckeys(self, region, kc, c0, c1, w=None):
        w = T if w is None else w
        return region.keys(kc * w + c0, kc * w + c1)

    def store_fm(self, dram_name, region, K, c0=0, c1=None, cols=None, q="q_sp"):
        cols = T if cols is None else cols
        if c1 is None:
            c1 = cols
        w = c1 - c0
        kc = K // 128
        dst = self.dram_t[dram_name]
        dv = dst.rearrange("(c p) t -> p c t", p=128)[:, :, c0:c1]
        sv = region.ap().rearrange("p (c t) -> p c t", c=kc)
        self.em.dma(q, dv, sv, reads=region.keys(),
                    writes=dram_keys(dram_name, K, 0, c0, c1))

    def ln(self, src_name, dst_name, eps=1e-5, gain=None, bias=None,
           dst32_name=None):
        em = self.em
        p = self
        self.af32.reset(); self.abf.reset(); self.asm.reset()
        gb = None
        if gain is not None:
            gb = self.asm.alloc(16)
            em.dma("q_sp", gb.ap(0, 128, 0, 8),
                   self.dram_t[gain].rearrange("(mc p) -> p mc", p=128),
                   reads=[("D", gain)], writes=gb.keys(),
                   allow_slow_non_contiguous=True)
            bb = self.asm.alloc(16)
            em.dma("q_sp", bb.ap(0, 128, 0, 8),
                   self.dram_t[bias].rearrange("(mc p) -> p mc", p=128),
                   reads=[("D", bias)], writes=bb.keys(),
                   allow_slow_non_contiguous=True)
        xf = self.af32.alloc(4096)
        xb = self.abf.alloc(4096)
        xq = self.abf.alloc(4096)
        xnb = self.abf.alloc(4096)
        tmp = self.af32.alloc(2048 if gain is not None else 1024)
        x32 = self.af32.alloc(4096) if dst32_name is not None else None
        rows = [self.asm.alloc(512) for _ in range(8)]
        mu, ex2, musq, var, vare, sd, rstd, mur = rows
        src = self.dram_t[src_name]
        for tt in range(T // 512):
            t0 = tt * 512
            for kc in range(8):
                em.dma("q_sp", xf.ap(0, 128, kc * 512, kc * 512 + 512),
                       src[kc * 128:(kc + 1) * 128, t0:t0 + 512],
                       reads=dram_keys(src_name, 128, kc * 128, t0, t0 + 512),
                       writes=xf.keys(kc * 512, kc * 512 + 512))
            for kc in range(8):
                a, b = kc * 512, kc * 512 + 512
                em.op("dve", lambda e, a=a, b=b: e.tensor_copy(xb.ap(0, 128, a, b), xf.ap(0, 128, a, b)),
                      reads=xf.keys(a, b), writes=xb.keys(a, b))
                em.op("act", lambda e, a=a, b=b: e.activation(out=xq.ap(0, 128, a, b), in_=xf.ap(0, 128, a, b), func=AF.Square),
                      reads=xf.keys(a, b), writes=xq.keys(a, b))
            for kc in range(8):
                a, b = kc * 512, kc * 512 + 512
                em.op("pe", lambda e, a=a, b=b, s=(kc == 0), p_=(kc == 7):
                      e.matmul(self.ps[4][0:1, :], self.const("c_ones_col_bf"), xb.ap(0, 128, a, b), start=s, stop=p_),
                      reads=xb.keys(a, b) + [("const", "c_ones_col_bf")], writes=["ps4"], inc=(kc == 7))
            for kc in range(8):
                a, b = kc * 512, kc * 512 + 512
                em.op("pe", lambda e, a=a, b=b, s=(kc == 0), p_=(kc == 7):
                      e.matmul(self.ps[5][0:1, :], self.const("c_ones_col_bf"), xq.ap(0, 128, a, b), start=s, stop=p_),
                      reads=xq.keys(a, b) + [("const", "c_ones_col_bf")], writes=["ps5"], inc=(kc == 7))
            em.op("dve", lambda e: e.tensor_scalar_mul(mu.ap(0, 1), self.ps[4][0:1, :], 1.0 / D),
                  reads=["ps4"], writes=mu.keys())
            em.op("dve", lambda e: e.tensor_scalar_mul(ex2.ap(0, 1), self.ps[5][0:1, :], 1.0 / D),
                  reads=["ps5"], writes=ex2.keys())
            em.op("act", lambda e: e.activation(out=musq.ap(0, 1), in_=mu.ap(0, 1), func=AF.Square),
                  reads=mu.keys(), writes=musq.keys())
            em.op("dve", lambda e: e.tensor_tensor(out=var.ap(0, 1), in0=ex2.ap(0, 1), in1=musq.ap(0, 1), op=OP.subtract),
                  reads=ex2.keys() + musq.keys(), writes=var.keys())
            em.op("dve", lambda e: e.tensor_scalar_add(vare.ap(0, 1), var.ap(0, 1), float(eps)),
                  reads=var.keys(), writes=vare.keys())
            em.op("act", lambda e: e.activation(out=sd.ap(0, 1), in_=vare.ap(0, 1), func=AF.Sqrt),
                  reads=vare.keys(), writes=sd.keys())
            em.op("dve", lambda e: e.reciprocal(rstd.ap(0, 1), sd.ap(0, 1)),
                  reads=sd.keys(), writes=rstd.keys())
            em.op("dve", lambda e: e.tensor_tensor(out=mur.ap(0, 1), in0=mu.ap(0, 1), in1=rstd.ap(0, 1), op=OP.mult),
                  reads=mu.keys() + rstd.keys(), writes=mur.keys())
            if "ln_dbg" in Prog.dbg_names:
                dbg = self.dram("ln_dbg", [8, T], F32)
                for ri, rr in enumerate(rows):
                    em.dma("q_sp", dbg[ri:ri + 1, t0:t0 + 512], rr.ap(0, 1),
                           reads=rr.keys(), writes=[("D", "ln_dbg")])
            em.op("pe", lambda e: e.matmul(self.ps[4][:, :], self.const("c_ones_row_f"), rstd.ap(0, 1), start=True, stop=True),
                  reads=rstd.keys() + [("const", "c_ones_row_f")], writes=["ps4"])
            em.op("pe", lambda e: e.matmul(self.ps[5][:, :], self.const("c_ones_row_f"), mur.ap(0, 1), start=True, stop=True),
                  reads=mur.keys() + [("const", "c_ones_row_f")], writes=["ps5"])
            for kc in range(8):
                a, b = kc * 512, kc * 512 + 512
                tp = (kc % 2) * 512
                em.op("dve", lambda e, a=a, b=b, tp=tp: e.tensor_tensor(out=tmp.ap(0, 128, tp, tp + 512), in0=xf.ap(0, 128, a, b), in1=self.ps[4][:, :], op=OP.mult),
                      reads=xf.keys(a, b) + ["ps4"], writes=tmp.keys(tp, tp + 512))
                if gb is None:
                    em.op("dve", lambda e, a=a, b=b, tp=tp: e.tensor_tensor(out=xnb.ap(0, 128, a, b), in0=tmp.ap(0, 128, tp, tp + 512), in1=self.ps[5][:, :], op=OP.subtract),
                          reads=tmp.keys(tp, tp + 512) + ["ps5"], writes=xnb.keys(a, b))
                else:
                    tq = 1024 + (kc % 2) * 512
                    em.op("dve", lambda e, a=a, b=b, tp=tp, tq=tq: e.tensor_tensor(out=tmp.ap(0, 128, tq, tq + 512), in0=tmp.ap(0, 128, tp, tp + 512), in1=self.ps[5][:, :], op=OP.subtract),
                          reads=tmp.keys(tp, tp + 512) + ["ps5"], writes=tmp.keys(tq, tq + 512))
                    em.op("dve", lambda e, a=a, b=b, tq=tq, kc=kc: e.tensor_scalar(
                        out=xnb.ap(0, 128, a, b), in0=tmp.ap(0, 128, tq, tq + 512),
                        scalar1=gb.ap(0, 128, kc, kc + 1), scalar2=bb.ap(0, 128, kc, kc + 1),
                        op0=OP.mult, op1=OP.add),
                        reads=tmp.keys(tq, tq + 512) + gb.keys() + bb.keys(), writes=xnb.keys(a, b))
                if dst32_name is not None:
                    em.op("act", lambda e, a=a, b=b: e.activation(out=x32.ap(0, 128, a, b), in_=xnb.ap(0, 128, a, b), func=AF.Identity),
                          reads=xnb.keys(a, b), writes=x32.keys(a, b))
            dst = self.dram_t[dst_name]
            dv = dst.rearrange("(c p) t -> p c t", p=128)[:, :, t0:t0 + 512]
            sv = xnb.ap().rearrange("p (c t) -> p c t", c=8)
            em.dma("q_sp", dv, sv, reads=xnb.keys(),
                   writes=dram_keys(dst_name, D, 0, t0, t0 + 512))
            if dst32_name is not None:
                d32 = self.dram_t[dst32_name]
                dv2 = d32.rearrange("(c p) t -> p c t", p=128)[:, :, t0:t0 + 512]
                sv2 = x32.ap().rearrange("p (c t) -> p c t", c=8)
                em.dma("q_act", dv2, sv2, reads=x32.keys(),
                       writes=dram_keys(dst32_name, D, 0, t0, t0 + 512))

    def proj_fm(self, dst_name, src, wname, K, M, bias=None, actf=None,
                scale=None, out_dtype=BF16, Tc=None, dst_region=None, dst_w=None,
                w_src_rows=None, accum=False, post_evac=None):
        Tc = T if Tc is None else Tc
        _off_bf, _off_f32, _off_sm = self.abf.off, self.af32.off, self.asm.off
        """dst[fm M x Tc] = act(scale*(src @ W) + bias); W natural [K, M] in DRAM.

        src: resident chunk-major Region (bf16, width Tc).
        dst: DRAM name (fm layout [M, Tc]) or dst_region (+dst_w row-chunk width).
        w_src_rows: optional row slice (r0) of a taller DRAM weight tensor.
        accum: DMA store accumulates into dst (SWDGE add).
        """
        em = self.em
        kc_n = K // 128
        wsl = [self.abf.alloc(K) for _ in range(2)]
        sar = self.af32 if out_dtype == F32 else self.abf
        stage = [sar.alloc(1024) for _ in range(2)]
        bias_sb = None
        if bias is not None:
            bias_sb = self.asm.alloc(M // 128)
            bap = self.dram_t[bias]
            em.dma("q_sp", bias_sb.ap(0, 128, 0, M // 128),
                   bap.rearrange("(mc p) -> p mc", p=128),
                   reads=[("D", bias)], writes=bias_sb.keys(),
                   allow_slow_non_contiguous=True)
        w = self.dram_t[wname]
        r0 = w_src_rows or 0
        for m in range(M // 128):
            sl = wsl[m % 2]
            wv = w.rearrange("(c p) m -> p c m", p=128)[:, r0 // 128 + 0:r0 // 128 + kc_n, m * 128:(m + 1) * 128] if False else \
                w[r0:r0 + K, m * 128:(m + 1) * 128].rearrange("(c p) m -> p c m", p=128)
            em.dma("q_pool", sl.ap().rearrange("p (c m) -> p c m", c=kc_n), wv,
                   reads=[("D", wname)], writes=sl.keys())
            for tg in range((Tc + 1023) // 1024):
                un = 2 if Tc - tg * 1024 >= 1024 else 1
                pb = (tg % 2) * 2
                for kc in range(kc_n):
                    for u in range(un):
                        t0 = tg * 1024 + u * 512
                        last = (kc == kc_n - 1 and u == un - 1)
                        em.op("pe", lambda e, kc=kc, t0=t0, s=(kc == 0), pb=pb, u=u, sl=sl:
                              e.matmul(self.ps[pb + u][:, :], sl.ap(0, 128, kc * 128, kc * 128 + 128),
                                       src.ap(0, 128, kc * Tc + t0, kc * Tc + t0 + 512),
                                       start=s, stop=(kc == kc_n - 1)),
                              reads=sl.keys(kc * 128, kc * 128 + 128) + src.keys(kc * Tc + t0, kc * Tc + t0 + 512),
                              writes=[f"ps{pb + u}"], inc=last)
                st = stage[tg % 2]
                for u in range(un):
                    bank = f"ps{pb + u}"
                    if post_evac is not None:
                        post_evac(self, m, tg, u, self.ps[pb + u], st, bank)
                        continue
                    bap2 = bias_sb.ap(0, 128, m, m + 1) if bias_sb is not None else None
                    bk = bias_sb.keys() if bias_sb is not None else []
                    if actf is not None:
                        if scale is not None:
                            sc_ap, sck = self.sconst(float(scale))
                        else:
                            sc_ap, sck = 1.0, []
                        em.op("act", lambda e, u=u, bank=bank, bap2=bap2, st=st, sc_ap=sc_ap:
                              e.activation(out=st.ap(0, 128, u * 512, u * 512 + 512), in_=self.ps[int(bank[2])][:, :],
                                           func=actf, bias=(bap2 if bap2 is not None else 0.0),
                                           scale=sc_ap),
                              reads=[bank] + bk + sck, writes=st.keys(u * 512, u * 512 + 512))
                    elif bias is not None:
                        if scale is not None:
                            em.op("dve", lambda e, u=u, bank=bank, bap2=bap2, st=st:
                                  e.tensor_scalar(out=st.ap(0, 128, u * 512, u * 512 + 512),
                                                  in0=self.ps[int(bank[2])][:, :], scalar1=float(scale),
                                                  scalar2=bap2, op0=OP.mult, op1=OP.add),
                                  reads=[bank] + bk, writes=st.keys(u * 512, u * 512 + 512))
                        else:
                            em.op("dve", lambda e, u=u, bank=bank, bap2=bap2, st=st:
                                  e.tensor_scalar_add(st.ap(0, 128, u * 512, u * 512 + 512),
                                                      self.ps[int(bank[2])][:, :], bap2),
                                  reads=[bank] + bk, writes=st.keys(u * 512, u * 512 + 512))
                    elif scale is not None:
                        em.op("dve", lambda e, u=u, bank=bank, st=st:
                              e.tensor_scalar_mul(st.ap(0, 128, u * 512, u * 512 + 512),
                                                  self.ps[int(bank[2])][:, :], float(scale)),
                              reads=[bank], writes=st.keys(u * 512, u * 512 + 512))
                    else:
                        em.op("dve", lambda e, u=u, bank=bank, st=st:
                              e.tensor_copy(st.ap(0, 128, u * 512, u * 512 + 512), self.ps[int(bank[2])][:, :]),
                              reads=[bank], writes=st.keys(u * 512, u * 512 + 512))
                if post_evac is not None:
                    continue
                tw = un * 512
                if dst_region is not None:
                    em.op("dve", lambda e, m=m, tg=tg, st=st, tw=tw: e.tensor_copy(
                        dst_region.ap(0, 128, m * dst_w + tg * 1024, m * dst_w + tg * 1024 + tw), st.ap(0, 128, 0, tw)),
                        reads=st.keys(0, tw), writes=dst_region.keys(m * dst_w + tg * 1024, m * dst_w + tg * 1024 + tw))
                else:
                    dst = self.dram_t[dst_name]
                    kw = {"accum_op": OP.add} if accum else {}
                    q = "q_pool" if accum else "q_sp"
                    em.dma(q, dst[m * 128:(m + 1) * 128, tg * 1024:tg * 1024 + tw], st.ap(0, 128, 0, tw),
                           reads=st.keys(0, tw), writes=dram_keys(dst_name, 128, m * 128, tg * 1024, tg * 1024 + tw), **kw)
        self.abf.off, self.af32.off, self.asm.off = _off_bf, _off_f32, _off_sm

    def proj_tm(self, dst_name, src, wname, K, N, bias=None, actf=None,
                scale=None, Tc=None, w_dtype=BF16, out_dtype=BF16):
        Tc = T if Tc is None else Tc
        _off_bf, _off_f32, _off_sm = self.abf.off, self.af32.off, self.asm.off
        """dst[tm Tc x N] = act(scale*(src @ W) + bias); W natural [K, N]."""
        em = self.em
        kc_n = K // 128
        wsb = self.abf.alloc(kc_n * N)
        w = self.dram_t[wname]
        em.dma("q_pool", wsb.ap().rearrange("p (c n) -> p c n", c=kc_n),
               w.rearrange("(c p) n -> p c n", p=128),
               reads=[("D", wname)], writes=wsb.keys())
        brow = None
        if bias is not None:
            brow = self.asm.alloc(N)
            em.dma("q_sp", brow.ap(0, 1, 0, N), self.dram_t[bias][None, :],
                   reads=[("D", bias)], writes=brow.keys())
        sar = self.af32 if out_dtype == F32 else self.abf
        stage = [sar.alloc(N) for _ in range(2)]
        dst = self.dram_t[dst_name]
        for tc in range(Tc // 128):
            st = stage[tc % 2]
            for nt in range((N + 511) // 512):
                nn = min(512, N - nt * 512)
                bank = f"ps{(tc % 2) * 2 + nt % 2}"
                bi = int(bank[2])
                for kc in range(kc_n):
                    lastmm = (kc == kc_n - 1 and bias is None)
                    em.op("pe", lambda e, kc=kc, nt=nt, tc=tc, bi=bi, nn=nn:
                          e.matmul(self.ps[bi][:, 0:nn], src.ap(0, 128, kc * Tc + tc * 128, kc * Tc + tc * 128 + 128),
                                   wsb.ap(0, 128, kc * N + nt * 512, kc * N + nt * 512 + nn),
                                   start=(kc == 0), stop=(kc == kc_n - 1 and bias is None)),
                          reads=src.keys(kc * Tc + tc * 128, kc * Tc + tc * 128 + 128) + wsb.keys(kc * N + nt * 512, kc * N + nt * 512 + nn),
                          writes=[bank], inc=lastmm)
                if bias is not None:
                    em.op("pe", lambda e, nt=nt, bi=bi, nn=nn:
                          e.matmul(self.ps[bi][:, 0:nn], self.const("c_ones_row_f"), brow.ap(0, 1, nt * 512, nt * 512 + nn),
                                   start=False, stop=True),
                          reads=brow.keys(nt * 512, nt * 512 + nn) + [("const", "c_ones_row_f")],
                          writes=[bank], inc=True)
                if actf is not None or scale is not None:
                    em.op("act", lambda e, nt=nt, bi=bi, st=st, nn=nn: e.activation(
                        out=st.ap(0, 128, nt * 512, nt * 512 + nn), in_=self.ps[bi][:, 0:nn],
                        func=(actf or AF.Copy), scale=(scale if scale is not None else 1.0)),
                        reads=[bank], writes=st.keys(nt * 512, nt * 512 + nn))
                else:
                    em.op("dve", lambda e, nt=nt, bi=bi, st=st, nn=nn: e.tensor_copy(
                        st.ap(0, 128, nt * 512, nt * 512 + nn), self.ps[bi][:, 0:nn]),
                        reads=[bank], writes=st.keys(nt * 512, nt * 512 + nn))
            em.dma("q_sp",
                   dst[tc * 128:(tc + 1) * 128, :], st.ap(0, 128, 0, N),
                   reads=st.keys(), writes=dram_keys(dst_name, 128, 0, tc * N, (tc + 1) * N))
        self.abf.off, self.asm.off = _off_bf, _off_sm

    def transpose_weight(self, src_name, dst_name, R, C, perm_qkv=False):
        em = self.em
        src = self.dram_t[src_name]
        dst = self.dram(dst_name, [C, R], BF16)
        wrow = [self.abf.alloc(C) for _ in range(2)]
        stage = [self.abf.alloc(C) for _ in range(2)]
        for rc in range(R // 128):
            wr = wrow[rc % 2]
            if perm_qkv:
                comp, h = rc // 8, rc % 8
                sv = src.rearrange("(c dh s) m -> c s dh m", s=8, dh=128)[comp, h, :, :]
            else:
                sv = src[rc * 128:(rc + 1) * 128, :]
            em.dma("q_pool", wr.ap(0, 128, 0, C), sv,
                   reads=[("D", src_name)], writes=wr.keys())
            st = stage[rc % 2]
            for cb in range(C // 1024):
                ptb = f"pt{cb % 2}"
                for cc in range(8):
                    c0 = cb * 1024 + cc * 128
                    em.op("pe", lambda e, c0=c0, cc=cc, ptb=ptb:
                          e.transpose(self.pt[int(ptb[2])][:, cc * 128:(cc + 1) * 128],
                                      wr.ap(0, 128, c0, c0 + 128), self.const("c_ident_bf")),
                          reads=wr.keys(c0, c0 + 128) + [("const", "c_ident_bf")],
                          writes=[ptb], inc=(cc == 7))
                em.op("dve", lambda e, cb=cb, ptb=ptb:
                      e.tensor_copy(st.ap(0, 128, cb * 1024, (cb + 1) * 1024), self.pt[int(ptb[2])][:, :]),
                      reads=[ptb], writes=st.keys(cb * 1024, (cb + 1) * 1024))
            dv = dst.rearrange("(cc p) r -> p cc r", p=128)[:, :, rc * 128:(rc + 1) * 128]
            em.dma("q_sp", dv, st.ap().rearrange("p (cc m) -> p cc m", cc=C // 128),
                   reads=st.keys(), writes=[("D", dst_name)])

    def combine(self, x_name, s_name, f_name, out_name, out_f32_name=None):
        em = self.em
        self.af32.reset(); self.abf.reset(); self.asm.reset()
        xf = self.af32.alloc(2048)
        sf = self.af32.alloc(2048)
        ff = self.af32.alloc(2048)
        gf = self.af32.alloc(2048)
        g2 = self.af32.alloc(2048)
        of = self.af32.alloc(2048)
        xs, ss, fs = self.dram_t[x_name], self.dram_t[s_name], self.dram_t[f_name]
        os_ = self.dram_t[out_name]
        CW = min(2048, T)
        for i in range(8 * (T // CW)):
            kc, th = i // (T // CW), i % (T // CW)
            r0, c0 = kc * 128, th * CW
            for nm, reg, tsr in ((x_name, xf, xs), (s_name, sf, ss), (f_name, ff, fs)):
                em.dma("q_sp" if nm != f_name else "q_act", reg.ap(0, 128, 0, CW),
                       tsr[r0:r0 + 128, c0:c0 + CW],
                       reads=dram_keys(nm), writes=reg.keys())
            em.op("act", lambda e: e.activation(out=gf.ap(0, 128, 0, CW), in_=ff.ap(0, 128, 0, CW), func=AF.Silu),
                  reads=ff.keys(), writes=gf.keys())
            em.op("dve", lambda e: e.tensor_tensor(out=g2.ap(0, 128, 0, CW), in0=gf.ap(0, 128, 0, CW),
                                                   in1=sf.ap(0, 128, 0, CW), op=OP.mult),
                  reads=gf.keys() + sf.keys(), writes=g2.keys())
            em.op("dve", lambda e: e.tensor_tensor(out=of.ap(0, 128, 0, CW), in0=xf.ap(0, 128, 0, CW),
                                                   in1=g2.ap(0, 128, 0, CW), op=OP.add),
                  reads=g2.keys() + xf.keys(), writes=of.keys())
            em.dma("q_sp", os_[r0:r0 + 128, c0:c0 + CW], of.ap(0, 128, 0, CW),
                  reads=of.keys(), writes=dram_keys(out_name))

    def attn_l0(self, pre):
        em = self.em
        p = self
        self.dram("l0_qT", [D, T], BF16)
        self.dram("l0_kT", [D, T], BF16)
        self.dram("l0_vtok", [T, D], BF16)
        self.dram("l0_ot", [D, T], BF16)
        self.dram("s_blk", [D, T], F32)
        self.newphase()
        xn_r = self.load_fm("xn", D)
        self.proj_fm("l0_qT", xn_r, f"{pre}wq", K=D, M=D, bias=f"{pre}bq")
        self.proj_fm("l0_kT", xn_r, f"{pre}wk", K=D, M=D, bias=f"{pre}bk")
        self.proj_tm("l0_vtok", xn_r, f"{pre}wv", K=D, N=D, bias=f"{pre}bv")
        self.newphase()
        qh = [self.abf.alloc(T) for _ in range(2)]
        kh = [self.abf.alloc(T) for _ in range(2)]
        vh = [self.abf.alloc(T) for _ in range(2)]
        oh = [self.abf.alloc(T) for _ in range(2)]
        stb = [self.abf.alloc(512) for _ in range(2)]
        for h in range(8):
            qr, kr, vr, orr = qh[h % 2], kh[h % 2], vh[h % 2], oh[h % 2]
            qt, kt_, vt = self.dram_t["l0_qT"], self.dram_t["l0_kT"], self.dram_t["l0_vtok"]
            em.dma("q_sp", qr.ap(), qt[h * 128:(h + 1) * 128, :],
                   reads=dram_keys("l0_qT", 128, h * 128, 0, T), writes=qr.keys())
            em.dma("q_sp", kr.ap(), kt_[h * 128:(h + 1) * 128, :],
                   reads=dram_keys("l0_kT", 128, h * 128, 0, T), writes=kr.keys())
            em.dma("q_act", vr.ap().rearrange("p (rc n) -> p rc n", rc=T // 128),
                   vt.rearrange("(rc p) n -> p rc n", p=128)[:, :, h * 128:(h + 1) * 128],
                   reads=dram_keys("l0_vtok"), writes=vr.keys())
            for qg in range(T // 512):
                q0 = qg * 512
                def score(kt):
                    bank = f"ps{2 + (kt % 2) * 2}"
                    em.op("pe", lambda e, kt=kt, q0=q0, bank=bank, kr=kr, qr=qr: e.matmul(
                        self.ps[int(bank[2])][:, :], kr.ap(0, 128, kt * 128, kt * 128 + 128),
                        qr.ap(0, 128, q0, q0 + 512), start=True, stop=True),
                        reads=kr.keys(kt * 128, kt * 128 + 128) + qr.keys(q0, q0 + 512),
                        writes=[bank])
                def evac(kt):
                    bank = f"ps{2 + (kt % 2) * 2}"
                    sb = stb[kt % 2]
                    eng = "dve" if kt % 2 == 0 else "act"
                    if eng == "dve":
                        em.op(eng, lambda e, sb=sb, bank=bank: e.tensor_copy(sb.ap(), self.ps[int(bank[2])][:, :]),
                              reads=[bank], writes=sb.keys())
                    else:
                        em.op(eng, lambda e, sb=sb, bank=bank: e.activation(out=sb.ap(), in_=self.ps[int(bank[2])][:, :], func=AF.Copy),
                              reads=[bank], writes=sb.keys())
                def av(kt):
                    sb = stb[kt % 2]
                    em.op("pe", lambda e, kt=kt, sb=sb, vr=vr: e.matmul(
                        self.ps[5][:, :], vr.ap(0, 128, kt * 128, kt * 128 + 128), sb.ap(),
                        start=(kt == 0), stop=(kt == T // 128 - 1)),
                        reads=vr.keys(kt * 128, kt * 128 + 128) + sb.keys(),
                        writes=["ps5"], inc=(kt == T // 128 - 1))
                score(0)
                for kt in range(1, T // 128):
                    score(kt)
                    evac(kt - 1)
                    av(kt - 1)
                evac(T // 128 - 1)
                av(T // 128 - 1)
                em.op("dve", lambda e, q0=q0, orr=orr: e.tensor_copy(orr.ap(0, 128, q0, q0 + 512), self.ps[5][:, :]),
                      reads=["ps5"], writes=orr.keys(q0, q0 + 512))
            em.dma("q_sp", self.dram_t["l0_ot"][h * 128:(h + 1) * 128, :], orr.ap(),
                   reads=orr.keys(), writes=dram_keys("l0_ot", 128, h * 128, 0, T))
        self.newphase()
        ot_r = self.load_fm("l0_ot", D)
        self.proj_fm("s_blk", ot_r, f"{pre}wo", K=D, M=D, bias=f"{pre}bo", out_dtype=F32)

    def newphase(self):
        self.abf.reset()
        self.af32.reset()
        self.asm.reset()

    def ffn_l0(self, pre):
        self.dram("l0_P", [256, T], BF16)
        self.dram("l0_H", [4096, T], BF16)
        self.dram("l0_Q32", [256, T], F32)
        self.dram("l0_Q", [256, T], BF16)
        self.dram("f_blk", [D, T], F32)
        self.newphase()
        xn_r = self.load_fm("xn", D)
        self.proj_fm("l0_P", xn_r, f"{pre}U1", K=D, M=256)
        self.newphase()
        p_r = self.load_fm("l0_P", 256)
        self.proj_fm("l0_H", p_r, f"{pre}V1", K=256, M=4096, bias=f"{pre}b1", actf=AF.Gelu_apprx_tanh)
        for kg in range(4):
            self.newphase()
            h_r = self._load_fm_rows("l0_H", 4096, kg * 1024, 1024)
            self.proj_fm("l0_Q32", h_r, f"{pre}U2", K=1024, M=256, out_dtype=F32,
                         w_src_rows=kg * 1024, accum=(kg > 0))
        self.newphase()
        q_r = self.load_fm("l0_Q32", 256)
        self.proj_fm("f_blk", q_r, f"{pre}V2", K=256, M=D, bias=f"{pre}b2", out_dtype=F32)

    def _load_fm_rows(self, dram_name, Ktot, r0, K, cols=None):
        cols = T if cols is None else cols
        """Load a row-slice [r0, r0+K) of a tall fm DRAM tensor, chunk-major."""
        kc = K // 128
        r = self.abf.alloc(kc * cols)
        src = self.dram_t[dram_name]
        sv = src[r0:r0 + K, :].rearrange("(c p) t -> p c t", p=128)
        dv = r.ap().rearrange("p (c t) -> p c t", c=kc)
        q = "q_pool" if src.dtype != BF16 else "q_sp"
        self.em.dma(q, dv, sv, reads=dram_keys(dram_name, K, r0, 0, cols), writes=r.keys())
        return r

    def ew_mul(self, dst_name, a_name, b_name, rows):
        em = self.em
        _obf = self.abf.off
        cw = min(2048, T)
        ar = [self.abf.alloc(cw) for _ in range(2)]
        br = [self.abf.alloc(cw) for _ in range(2)]
        dr = [self.abf.alloc(cw) for _ in range(2)]
        ad, bd, dd = self.dram_t[a_name], self.dram_t[b_name], self.dram_t[dst_name]
        n = 0
        for rc in range(rows // 128):
            for cb in range(T // cw):
                aa, bb, ddr = ar[n % 2], br[n % 2], dr[n % 2]
                em.dma("q_sp", aa.ap(0, 128, 0, cw), ad[rc * 128:(rc + 1) * 128, cb * cw:(cb + 1) * cw],
                       reads=dram_keys(a_name), writes=aa.keys())
                em.dma("q_act", bb.ap(0, 128, 0, cw), bd[rc * 128:(rc + 1) * 128, cb * cw:(cb + 1) * cw],
                       reads=dram_keys(b_name), writes=bb.keys())
                em.op("dve", lambda e, aa=aa, bb=bb, ddr=ddr, cw=cw: e.tensor_tensor(
                    out=ddr.ap(0, 128, 0, cw), in0=aa.ap(0, 128, 0, cw),
                    in1=bb.ap(0, 128, 0, cw), op=OP.mult),
                    reads=aa.keys() + bb.keys(), writes=ddr.keys())
                em.dma("q_sp", dd[rc * 128:(rc + 1) * 128, cb * cw:(cb + 1) * cw], ddr.ap(0, 128, 0, cw),
                       reads=ddr.keys(), writes=dram_keys(dst_name))
                n += 1
        self.abf.off = _obf

    def attn_l2(self):
        em = self.em
        NH, hd, LAT = 16, 64, 512
        scale = 0.125
        self.newphase()
        GQ = 256
        nqg = T // GQ
        ckv_fm = self.load_fm("l2_ckv", LAT)
        ckv_tok = self.abf.alloc((T // 128) * LAT)
        vt = self.dram_t["l2_ckv_tok"]
        em.dma("q_sp", ckv_tok.ap().rearrange("p (rc n) -> p rc n", n=LAT),
               vt.rearrange("(rc p) n -> p rc n", p=128),
               reads=dram_keys("l2_ckv_tok"), writes=ckv_tok.keys())
        qh = self.abf.alloc(T)
        qlat = self.abf.alloc(4 * GQ)
        keff = self.abf.alloc(LAT)
        veff = self.abf.alloc(4 * hd)
        P = self.abf.alloc(T)
        Pn = self.abf.alloc(T)
        PT = self.abf.alloc((T // 128) * GQ)
        ylat = self.abf.alloc(4 * GQ)
        S = [self.af32.alloc(T) for _ in range(2)]
        oh = self.af32.alloc(T)
        mcol = [self.asm.alloc(1) for _ in range(2)]
        bcol = [self.asm.alloc(1) for _ in range(2)]
        scol = [self.asm.alloc(1) for _ in range(2)]
        rcol = [self.asm.alloc(1) for _ in range(2)]
        sc_ap, sc_keys = self.sconst(scale)
        qd, kfd, vfd, od = (self.dram_t["l2_q"], self.dram_t["l2_keff"],
                            self.dram_t["l2_veff"], self.dram_t["s_blk"])
        for h in range(NH):
            r0 = h * hd
            em.dma("q_sp", qh.ap(0, hd, 0, T), qd[r0:r0 + hd, :],
                   reads=dram_keys("l2_q"), writes=qh.keys())
            em.dma("q_sp", keff.ap(0, hd, 0, LAT), kfd[r0:r0 + hd, :],
                   reads=dram_keys("l2_keff"), writes=keff.keys())
            em.dma("q_act", veff.ap(0, 128, 0, 4 * hd).rearrange("p (c n) -> p c n", c=4),
                   vfd.rearrange("(c p) n -> p c n", p=128)[:, :, r0:r0 + hd],
                   reads=dram_keys("l2_veff"), writes=veff.keys())
            for qg in range(nqg):
                kwg = (qg + 1) * GQ
                q0g = qg * GQ
                for lc in range(4):
                    em.op("pe", lambda e, lc=lc, q0g=q0g: e.matmul(
                        self.ps[5][:, 0:GQ], keff.ap(0, hd, lc * 128, (lc + 1) * 128),
                        qh.ap(0, hd, q0g, q0g + GQ), start=True, stop=True),
                        reads=keff.keys(lc * 128, (lc + 1) * 128) + qh.keys(q0g, q0g + GQ),
                        writes=["ps5"])
                    em.op("dve", lambda e, lc=lc: e.tensor_copy(
                        qlat.ap(0, 128, lc * GQ, (lc + 1) * GQ), self.ps[5][:, 0:GQ]),
                        reads=["ps5"], writes=qlat.keys(lc * GQ, (lc + 1) * GQ))
                for j in range(GQ // 128):
                    q0 = q0g + j * 128
                    kw = q0 + 128
                    nkb = (kw + 511) // 512
                    sbank = f"ps{3 + (j % 2)}"
                    sb = int(sbank[2])
                    Sb = S[j % 2]
                    for kb in range(nkb):
                        n = min(512, kw - kb * 512)
                        for lc in range(4):
                            em.op("pe", lambda e, kb=kb, n=n, lc=lc, j=j, sb=sb: e.matmul(
                                self.ps[sb][:, 0:n], qlat.ap(0, 128, lc * GQ + j * 128, lc * GQ + (j + 1) * 128),
                                ckv_fm.ap(0, 128, lc * T + kb * 512, lc * T + kb * 512 + n),
                                start=(lc == 0), stop=(lc == 3)),
                                reads=qlat.keys(lc * GQ + j * 128, lc * GQ + (j + 1) * 128) + ckv_fm.keys(lc * T + kb * 512, lc * T + kb * 512 + n),
                                writes=[sbank], inc=(lc == 3))
                        plain_n = (n - 128) if kb == nkb - 1 else n
                        eng = "dve" if kb % 2 == 0 else "act"
                        if plain_n > 0:
                            if eng == "dve":
                                em.op(eng, lambda e, kb=kb, pn=plain_n, sb=sb, Sb=Sb: e.tensor_copy(
                                    Sb.ap(0, 128, kb * 512, kb * 512 + pn), self.ps[sb][:, 0:pn]),
                                    reads=[sbank], writes=Sb.keys(kb * 512, kb * 512 + plain_n))
                            else:
                                em.op(eng, lambda e, kb=kb, pn=plain_n, sb=sb, Sb=Sb: e.activation(
                                    out=Sb.ap(0, 128, kb * 512, kb * 512 + pn), in_=self.ps[sb][:, 0:pn], func=AF.Identity),
                                    reads=[sbank], writes=Sb.keys(kb * 512, kb * 512 + plain_n))
                        em.op("dve", lambda e, q0=q0, kb=kb, n=n, sb=sb, Sb=Sb: e.tensor_tensor(
                            out=Sb.ap(0, 128, q0, q0 + 128), in0=self.ps[sb][:, n - 128:n],
                            in1=self.const("c_causal"), op=OP.add),
                            reads=[sbank, ("const", "c_causal")], writes=Sb.keys(q0, q0 + 128)) if kb == nkb - 1 else None
                    em.op("dve", lambda e, kw=kw, j=j, Sb=Sb: e.reduce_max(
                        mcol[j].ap(0, 128, 0, 1), Sb.ap(0, 128, 0, kw), axis=AX),
                        reads=Sb.keys(0, kw), writes=mcol[j].keys())
                    em.op("dve", lambda e, j=j: e.tensor_scalar_mul(
                        bcol[j].ap(0, 128, 0, 1), mcol[j].ap(0, 128, 0, 1), -scale),
                        reads=mcol[j].keys(), writes=bcol[j].keys())
                    em.op("act", lambda e, kw=kw, j=j, Sb=Sb: e.activation(
                        out=P.ap(0, 128, 0, kw), in_=Sb.ap(0, 128, 0, kw), func=AF.Exp,
                        bias=bcol[j].ap(0, 128, 0, 1), scale=sc_ap,
                        accum_out=scol[j].ap(0, 128, 0, 1)),
                        reads=Sb.keys(0, kw) + bcol[j].keys() + sc_keys,
                        writes=P.keys(0, kw) + scol[j].keys())
                    em.op("dve", lambda e, j=j: e.reciprocal(
                        rcol[j].ap(0, 128, 0, 1), scol[j].ap(0, 128, 0, 1)),
                        reads=scol[j].keys(), writes=rcol[j].keys())
                    em.op("dve", lambda e, kw=kw, j=j: e.tensor_scalar_mul(
                        Pn.ap(0, 128, 0, kw), P.ap(0, 128, 0, kw), rcol[j].ap(0, 128, 0, 1)),
                        reads=P.keys(0, kw) + rcol[j].keys(), writes=Pn.keys(0, kw))
                    if kw < kwg:
                        em.op("dve", lambda e, kw=kw, kwg=kwg: e.memset(
                            Pn.ap(0, 128, kw, kwg), 0.0),
                            reads=[], writes=Pn.keys(kw, kwg))
                    if "l2_dbg" in Prog.dbg_names and h == 0:
                        dbgp = self.dram("l2_dbg_pn", [T, T], BF16)
                        dbgs = self.dram("l2_dbg_s", [T, T], F32)
                        em.dma("q_sp", dbgp[q0:q0 + 128, 0:kwg], Pn.ap(0, 128, 0, kwg),
                               reads=Pn.keys(0, kwg), writes=[("D", "l2_dbg_pn")])
                        em.dma("q_sp", dbgs[q0:q0 + 128, 0:kw], Sb.ap(0, 128, 0, kw),
                               reads=Sb.keys(0, kw), writes=[("D", "l2_dbg_s")])
                    for rc in range(kwg // 128):
                        ptb = f"pt{rc % 2}"
                        em.op("pe", lambda e, rc=rc, j=j, ptb=ptb: e.transpose(
                            self.pt[int(ptb[2])][:, j * 128:(j + 1) * 128],
                            Pn.ap(0, 128, rc * 128, rc * 128 + 128), self.const("c_ident_bf")),
                            reads=Pn.keys(rc * 128, rc * 128 + 128) + [("const", "c_ident_bf")],
                            writes=[ptb], inc=True)
                        em.op("dve", lambda e, rc=rc, j=j, ptb=ptb: e.tensor_copy(
                            PT.ap(0, 128, rc * GQ + j * 128, rc * GQ + (j + 1) * 128),
                            self.pt[int(ptb[2])][:, j * 128:(j + 1) * 128]),
                            reads=[ptb], writes=PT.keys(rc * GQ + j * 128, rc * GQ + (j + 1) * 128))
                if "l2_dbg" in Prog.dbg_names and h == 0:
                    dbgt = self.dram("l2_dbg_pt", [T, 8 * T // 4], BF16)
                    nrc0 = kwg // 128
                    for rc in range(nrc0):
                        em.dma("q_sp", dbgt[rc * 128:(rc + 1) * 128, q0g:q0g + GQ],
                               PT.ap(0, 128, rc * GQ, (rc + 1) * GQ),
                               reads=PT.keys(rc * GQ, (rc + 1) * GQ), writes=[("D", "l2_dbg_pt")])
                nrc = kwg // 128
                for lc in range(4):
                    bank = ["ps0", "ps1", "ps2", "ps5"][lc]
                    bi = int(bank[2])
                    for rc in range(nrc):
                        em.op("pe", lambda e, rc=rc, lc=lc, bi=bi, nrc=nrc: e.matmul(
                            self.ps[bi][:, 0:GQ], ckv_tok.ap(0, 128, rc * LAT + lc * 128, rc * LAT + (lc + 1) * 128),
                            PT.ap(0, 128, rc * GQ, (rc + 1) * GQ),
                            start=(rc == 0), stop=(rc == nrc - 1)),
                            reads=ckv_tok.keys(rc * LAT + lc * 128, rc * LAT + (lc + 1) * 128) + PT.keys(rc * GQ, (rc + 1) * GQ),
                            writes=[bank], inc=(rc == nrc - 1))
                    em.op("dve", lambda e, lc=lc, bi=bi: e.tensor_copy(
                        ylat.ap(0, 128, lc * GQ, (lc + 1) * GQ), self.ps[bi][:, 0:GQ]),
                        reads=[bank], writes=ylat.keys(lc * GQ, (lc + 1) * GQ))
                if "l2_dbg" in Prog.dbg_names and h == 0:
                    dbgy = self.dram("l2_dbg_ylat", [128, 4 * T], BF16)
                    dbgq = self.dram("l2_dbg_qlat", [128, 4 * T], BF16)
                    for lc in range(4):
                        em.dma("q_sp", dbgy[:, lc * T + q0g:lc * T + q0g + GQ],
                               ylat.ap(0, 128, lc * GQ, (lc + 1) * GQ),
                               reads=ylat.keys(lc * GQ, (lc + 1) * GQ), writes=[("D", "l2_dbg_ylat")])
                        em.dma("q_sp", dbgq[:, lc * T + q0g:lc * T + q0g + GQ],
                               qlat.ap(0, 128, lc * GQ, (lc + 1) * GQ),
                               reads=qlat.keys(lc * GQ, (lc + 1) * GQ), writes=[("D", "l2_dbg_qlat")])
                for lc in range(4):
                    em.op("pe", lambda e, lc=lc: e.matmul(
                        self.ps[4][0:hd, 0:GQ], veff.ap(0, 128, lc * hd, (lc + 1) * hd),
                        ylat.ap(0, 128, lc * GQ, (lc + 1) * GQ),
                        start=(lc == 0), stop=(lc == 3)),
                        reads=veff.keys(lc * hd, (lc + 1) * hd) + ylat.keys(lc * GQ, (lc + 1) * GQ),
                        writes=["ps4"], inc=(lc == 3))
                eng = "dve" if qg % 2 == 0 else "act"
                if eng == "dve":
                    em.op(eng, lambda e, q0g=q0g: e.tensor_copy(
                        oh.ap(0, hd, q0g, q0g + GQ), self.ps[4][0:hd, 0:GQ]),
                        reads=["ps4"], writes=oh.keys(q0g, q0g + GQ))
                else:
                    em.op(eng, lambda e, q0g=q0g: e.activation(
                        out=oh.ap(0, hd, q0g, q0g + GQ), in_=self.ps[4][0:hd, 0:GQ], func=AF.Identity),
                        reads=["ps4"], writes=oh.keys(q0g, q0g + GQ))
            em.dma("q_sp", od[r0:r0 + hd, :], oh.ap(0, hd, 0, T),
                   reads=oh.keys(), writes=dram_keys("s_blk"))

    def router_top1(self, logits_name, ct_name):
        em = self.em
        cT = self.af32.alloc(T)
        lt = self.af32.alloc(16 * 2)
        msk = self.af32.alloc(16)
        mx = self.asm.alloc(1)
        ld = self.dram_t[logits_name]
        for tc in range(T // 128):
            sl = (tc % 2) * 16
            em.dma("q_sp", lt.ap(0, 128, sl, sl + 8), ld[tc * 128:(tc + 1) * 128, :],
                   reads=dram_keys(logits_name), writes=lt.keys(sl, sl + 8))
            em.op("dve", lambda e, sl=sl: e.reduce_max(mx.ap(0, 128, 0, 1), lt.ap(0, 128, sl, sl + 8), axis=AX),
                  reads=lt.keys(sl, sl + 8), writes=mx.keys())
            em.op("dve", lambda e, sl=sl: e.tensor_scalar(out=msk.ap(0, 128, 0, 8), in0=lt.ap(0, 128, sl, sl + 8),
                                                          scalar1=mx.ap(0, 128, 0, 1), scalar2=None, op0=OP.is_ge),
                  reads=lt.keys(sl, sl + 8) + mx.keys(), writes=msk.keys())
            em.op("pe", lambda e: e.transpose(self.ps[2][0:8, 0:128], msk.ap(0, 128, 0, 8), self.const("c_ident_f")),
                  reads=msk.keys() + [("const", "c_ident_f")], writes=["ps2"])
            em.op("dve", lambda e, tc=tc: e.tensor_copy(cT.ap(0, 8, tc * 128, (tc + 1) * 128), self.ps[2][0:8, 0:128]),
                  reads=["ps2"], writes=cT.keys(tc * 128, (tc + 1) * 128))
        cd = self.dram(ct_name, [8, T], F32)
        em.dma("q_sp", cd[:, :], cT.ap(0, 8, 0, T), reads=cT.keys(),
               writes=[("D", ct_name)])

    def moe_relu(self, xn_name, out_name, w1_name, b1_name, w2_name,
                 cT_name, crow, I=4096):
        em = self.em
        ffn = I // 128
        for tq in range(T // 1024):
            _obf, _o32, _osm = self.abf.off, self.af32.off, self.asm.off
            t0 = tq * 1024
            xn_q = self.load_fm(xn_name, D, c0=t0, c1=t0 + 1024)
            h = self.abf.alloc(ffn * 1024)
            _opass = self.abf.off
            w1sl = [self.abf.alloc(D) for _ in range(2)]
            b1c = self.asm.alloc(ffn)
            em.dma("q_sp", b1c.ap(0, 128, 0, ffn),
                   self.dram_t[b1_name][crow].rearrange("(mc p) -> p mc", p=128),
                   reads=[("D", b1_name)], writes=b1c.keys(),
                   allow_slow_non_contiguous=True)
            w1_d = self.dram_t[w1_name]
            for fm in range(ffn):
                sl = w1sl[fm % 2]
                em.dma("q_pool", sl.ap().rearrange("p (c m) -> p c m", c=8),
                       w1_d[:, fm * 128:(fm + 1) * 128].rearrange("(c p) m -> p c m", p=128),
                       reads=[("D", w1_name)], writes=sl.keys())
                for xk in range(8):
                    for u in range(2):
                        em.op("pe", lambda e, xk=xk, u=u, sl=sl: e.matmul(
                            self.ps[u][:, :], sl.ap(0, 128, xk * 128, (xk + 1) * 128),
                            xn_q.ap(0, 128, xk * 1024 + u * 512, xk * 1024 + u * 512 + 512),
                            start=(xk == 0), stop=(xk == 7)),
                            reads=sl.keys(xk * 128, (xk + 1) * 128) + xn_q.keys(xk * 1024 + u * 512, xk * 1024 + u * 512 + 512),
                            writes=[f"ps{u}"], inc=(xk == 7 and u == 1))
                for u in range(2):
                    em.op("act", lambda e, u=u, fm=fm: e.activation(
                        out=h.ap(0, 128, fm * 1024 + u * 512, fm * 1024 + u * 512 + 512),
                        in_=self.ps[u][:, :], func=AF.Relu, bias=b1c.ap(0, 128, fm, fm + 1)),
                        reads=[f"ps{u}"] + b1c.keys(), writes=h.keys(fm * 1024 + u * 512, fm * 1024 + u * 512 + 512))
            self.abf.off = _opass
            wdsl = [self.abf.alloc(I) for _ in range(2)]
            stage = [self.af32.alloc(1024) for _ in range(2)]
            bc_sb = self.af32.alloc(1024)
            crow_r = self.af32.alloc(1024)
            cd = self.dram_t[cT_name]
            em.dma("q_sp", crow_r.ap(0, 1, 0, 1024), cd[crow:crow + 1, t0:t0 + 1024],
                   reads=[("D", cT_name)], writes=crow_r.keys())
            for u in range(2):
                em.op("pe", lambda e, u=u, crow_r=crow_r: e.matmul(
                    self.ps[4][:, :], self.const("c_ones_row_f"),
                    crow_r.ap(0, 1, u * 512, u * 512 + 512), start=True, stop=True),
                    reads=crow_r.keys(u * 512, u * 512 + 512) + [("const", "c_ones_row_f")],
                    writes=["ps4"])
                em.op("dve", lambda e, u=u: e.tensor_copy(bc_sb.ap(0, 128, u * 512, u * 512 + 512), self.ps[4][:, :]),
                      reads=["ps4"], writes=bc_sb.keys(u * 512, u * 512 + 512))
            out_d = self.dram_t[out_name]
            w2_d = self.dram_t[w2_name]
            for m in range(8):
                sl = wdsl[m % 2]
                em.dma("q_pool", sl.ap().rearrange("p (c m) -> p c m", c=ffn),
                       w2_d[:, m * 128:(m + 1) * 128].rearrange("(c p) m -> p c m", p=128),
                       reads=[("D", w2_name)], writes=sl.keys())
                for fk in range(ffn):
                    for u in range(2):
                        em.op("pe", lambda e, fk=fk, u=u, sl=sl, ffn=ffn: e.matmul(
                            self.ps[u][:, :], sl.ap(0, 128, fk * 128, (fk + 1) * 128),
                            h.ap(0, 128, fk * 1024 + u * 512, fk * 1024 + u * 512 + 512),
                            start=(fk == 0), stop=(fk == ffn - 1)),
                            reads=sl.keys(fk * 128, (fk + 1) * 128) + h.keys(fk * 1024 + u * 512, fk * 1024 + u * 512 + 512),
                            writes=[f"ps{u}"], inc=(fk == ffn - 1 and u == 1))
                st = stage[m % 2]
                for u in range(2):
                    em.op("dve", lambda e, u=u, st=st: e.tensor_tensor(
                        out=st.ap(0, 128, u * 512, u * 512 + 512), in0=self.ps[u][:, :],
                        in1=bc_sb.ap(0, 128, u * 512, u * 512 + 512), op=OP.mult),
                        reads=[f"ps{u}"] + bc_sb.keys(u * 512, u * 512 + 512),
                        writes=st.keys(u * 512, u * 512 + 512))
                em.dma("q_pool", out_d[m * 128:(m + 1) * 128, t0:t0 + 1024], st.ap(0, 128, 0, 1024),
                       reads=st.keys() + [("D", out_name + "_zero")],
                       writes=[("D", out_name)], no_waw=True, accum_op=OP.add)
            self.abf.off, self.af32.off, self.asm.off = _obf, _o32, _osm

    def moe_b2(self, out_name, b2_name, cT_name):
        em = self.em
        _o32 = self.af32.off
        st = [self.af32.alloc(512) for _ in range(2)]
        b2d = self.dram_t[b2_name]
        cd = self.dram_t[cT_name]
        ctr = self.af32.alloc(512)
        out_d = self.dram_t[out_name]
        b2full = self.af32.alloc(1024)
        em.dma("q_sp", b2full.ap(0, 8, 0, 1024), b2d[:, :],
               reads=[("D", b2_name)], writes=b2full.keys())
        for m in range(8):
            for tt in range(T // 512):
                em.dma("q_sp", ctr.ap(0, 8, 0, 512), cd[:, tt * 512:(tt + 1) * 512],
                       reads=[("D", cT_name)], writes=ctr.keys())
                em.op("pe", lambda e, m=m: e.matmul(
                    self.ps[3][:, :], b2full.ap(0, 8, m * 128, (m + 1) * 128),
                    ctr.ap(0, 8, 0, 512), start=True, stop=True),
                    reads=b2full.keys(m * 128, (m + 1) * 128) + ctr.keys(), writes=["ps3"])
                s = st[tt % 2]
                em.op("dve", lambda e, s=s: e.tensor_copy(s.ap(0, 128, 0, 512), self.ps[3][:, :]),
                      reads=["ps3"], writes=s.keys())
                em.dma("q_pool", out_d[m * 128:(m + 1) * 128, tt * 512:(tt + 1) * 512], s.ap(0, 128, 0, 512),
                       reads=s.keys() + [("D", out_name + "_zero")],
                       writes=[("D", out_name)], no_waw=True, accum_op=OP.add)
        self.af32.off = _o32

    def attn_inner(self, qkv_name, vtok_name, ot_name):
        em = self.em
        self.newphase()
        H, hd = 4, 256
        scale = 1.0 / 16.0
        qh = self.abf.alloc(2 * T)
        kh = self.abf.alloc(2 * T)
        vh = self.abf.alloc((T // 128) * hd)
        oh = self.abf.alloc(2 * T)
        P = self.abf.alloc(128)
        Pn = self.abf.alloc(128)
        PT = self.abf.alloc(128)
        S = self.af32.alloc(128)
        mcol = self.asm.alloc(1)
        bcol = self.asm.alloc(1)
        scol = self.asm.alloc(1)
        rcol = self.asm.alloc(1)
        sc_ap, sc_keys = self.sconst(scale)
        qkvd = self.dram_t[qkv_name]
        vd = self.dram_t[vtok_name]
        od = self.dram_t[ot_name]
        for h in range(H):
            em.dma("q_sp", qh.ap().rearrange("p (c t) -> p c t", c=2),
                   qkvd[h * hd:(h + 1) * hd, :].rearrange("(c p) t -> p c t", p=128),
                   reads=dram_keys(qkv_name), writes=qh.keys())
            em.dma("q_sp", kh.ap().rearrange("p (c t) -> p c t", c=2),
                   qkvd[1024 + h * hd:1024 + (h + 1) * hd, :].rearrange("(c p) t -> p c t", p=128),
                   reads=dram_keys(qkv_name), writes=kh.keys())
            em.dma("q_act", vh.ap().rearrange("p (rc n) -> p rc n", n=hd),
                   vd.rearrange("(rc p) n -> p rc n", p=128)[:, :, h * hd:(h + 1) * hd],
                   reads=dram_keys(vtok_name), writes=vh.keys())
            for tc in range(T // 128):
                q0 = tc * 128
                for kc in range(2):
                    em.op("pe", lambda e, q0=q0, kc=kc: e.matmul(
                        self.ps[3][:, 0:128], qh.ap(0, 128, kc * T + q0, kc * T + q0 + 128),
                        kh.ap(0, 128, kc * T + q0, kc * T + q0 + 128),
                        start=(kc == 0), stop=(kc == 1)),
                        reads=qh.keys(kc * T + q0, kc * T + q0 + 128) + kh.keys(kc * T + q0, kc * T + q0 + 128),
                        writes=["ps3"], inc=(kc == 1))
                em.op("dve", lambda e: e.tensor_tensor(
                    out=S.ap(0, 128, 0, 128), in0=self.ps[3][:, 0:128],
                    in1=self.const("c_bdiag"), op=OP.add),
                    reads=["ps3", ("const", "c_bdiag")], writes=S.keys())
                em.op("dve", lambda e: e.reduce_max(mcol.ap(0, 128, 0, 1), S.ap(0, 128, 0, 128), axis=AX),
                      reads=S.keys(), writes=mcol.keys())
                em.op("dve", lambda e: e.tensor_scalar_mul(bcol.ap(0, 128, 0, 1), mcol.ap(0, 128, 0, 1), -scale),
                      reads=mcol.keys(), writes=bcol.keys())
                em.op("act", lambda e: e.activation(
                    out=P.ap(0, 128, 0, 128), in_=S.ap(0, 128, 0, 128), func=AF.Exp,
                    bias=bcol.ap(0, 128, 0, 1), scale=sc_ap, accum_out=scol.ap(0, 128, 0, 1)),
                    reads=S.keys() + bcol.keys() + sc_keys, writes=P.keys() + scol.keys())
                em.op("dve", lambda e: e.reciprocal(rcol.ap(0, 128, 0, 1), scol.ap(0, 128, 0, 1)),
                      reads=scol.keys(), writes=rcol.keys())
                em.op("dve", lambda e: e.tensor_scalar_mul(Pn.ap(0, 128, 0, 128), P.ap(0, 128, 0, 128), rcol.ap(0, 128, 0, 1)),
                      reads=P.keys() + rcol.keys(), writes=Pn.keys())
                ptb = f"pt{tc % 2}"
                em.op("pe", lambda e, ptb=ptb: e.transpose(
                    self.pt[int(ptb[2])][:, 0:128], Pn.ap(0, 128, 0, 128), self.const("c_ident_bf")),
                    reads=Pn.keys() + [("const", "c_ident_bf")], writes=[ptb], inc=True)
                em.op("dve", lambda e, ptb=ptb: e.tensor_copy(PT.ap(0, 128, 0, 128), self.pt[int(ptb[2])][:, 0:128]),
                      reads=[ptb], writes=PT.keys())
                for mc in range(2):
                    em.op("pe", lambda e, mc=mc, tc=tc: e.matmul(
                        self.ps[4][:, 0:128], vh.ap(0, 128, tc * hd + mc * 128, tc * hd + (mc + 1) * 128),
                        PT.ap(0, 128, 0, 128), start=True, stop=True),
                        reads=vh.keys(tc * hd + mc * 128, tc * hd + (mc + 1) * 128) + PT.keys(),
                        writes=["ps4"])
                    em.op("dve", lambda e, mc=mc, q0=q0: e.tensor_copy(
                        oh.ap(0, 128, mc * T + q0, mc * T + q0 + 128), self.ps[4][:, 0:128]),
                        reads=["ps4"], writes=oh.keys(mc * T + q0, mc * T + q0 + 128))
            em.dma("q_sp", od[h * hd:(h + 1) * hd, :].rearrange("(c p) t -> p c t", p=128),
                   oh.ap().rearrange("p (c t) -> p c t", c=2),
                   reads=oh.keys(), writes=dram_keys(ot_name))

    def ew_add(self, dst_name, a_name, b_name, rows, out_dtype=F32):
        em = self.em
        _obf, _o32 = self.abf.off, self.af32.off
        cw = min(2048, T)
        ar = [self.abf.alloc(cw) for _ in range(2)]
        br = [self.af32.alloc(cw) for _ in range(2)]
        dr = [self.af32.alloc(cw) for _ in range(2)]
        ad, bd, dd = self.dram_t[a_name], self.dram_t[b_name], self.dram_t[dst_name]
        n = 0
        for rc in range(rows // 128):
            for cb in range(T // cw):
                aa, bb, ddr = ar[n % 2], br[n % 2], dr[n % 2]
                em.dma("q_sp", aa.ap(0, 128, 0, cw), ad[rc * 128:(rc + 1) * 128, cb * cw:(cb + 1) * cw],
                       reads=dram_keys(a_name), writes=aa.keys())
                em.dma("q_act", bb.ap(0, 128, 0, cw), bd[rc * 128:(rc + 1) * 128, cb * cw:(cb + 1) * cw],
                       reads=dram_keys(b_name), writes=bb.keys())
                em.op("dve", lambda e, aa=aa, bb=bb, ddr=ddr, cw=cw: e.tensor_tensor(
                    out=ddr.ap(0, 128, 0, cw), in0=bb.ap(0, 128, 0, cw),
                    in1=aa.ap(0, 128, 0, cw), op=OP.add),
                    reads=aa.keys() + bb.keys(), writes=ddr.keys())
                em.dma("q_sp", dd[rc * 128:(rc + 1) * 128, cb * cw:(cb + 1) * cw], ddr.ap(0, 128, 0, cw),
                       reads=ddr.keys(), writes=dram_keys(dst_name))
                n += 1
        self.abf.off, self.af32.off = _obf, _o32

    def attn_causal(self, cfg):
        em = self.em
        NH, hd, scale = cfg.NH, cfg.hd, cfg.scale
        self.newphase()
        nqg = T // 512
        qh = self.abf.alloc(T)
        kh = self.abf.alloc(T)
        vh = self.abf.alloc((T // 128) * hd)
        oh = self.abf.alloc(T)
        P = [self.abf.alloc(T) for _ in range(2)]
        Pn = [self.abf.alloc(T) for _ in range(2)]
        PT = self.abf.alloc((T // 128) * 512)
        S = [self.af32.alloc(T) for _ in range(2)]
        mcol = [self.asm.alloc(1) for _ in range(4)]
        bcol = [self.asm.alloc(1) for _ in range(4)]
        scol = [self.asm.alloc(1) for _ in range(4)]
        rcol = [self.asm.alloc(1) for _ in range(4)]
        sc_ap, sc_keys = self.sconst(scale)
        rope = None
        if cfg.rope is not None:
            qrn, krn, rdim0 = cfg.rope
            qrr = self.abf.alloc(T)
            krr = self.abf.alloc(T)
            em.dma("q_sp", qrr.ap(0, rdim0, 0, T), self.dram_t[qrn][0:rdim0, :],
                   reads=dram_keys(qrn), writes=qrr.keys())
            em.dma("q_sp", krr.ap(0, rdim0, 0, T), self.dram_t[krn][0:rdim0, :],
                   reads=dram_keys(krn), writes=krr.keys())
            rope = (qrr, krr, rdim0)
        qd, kd, vd, od = (self.dram_t[cfg.qname], self.dram_t[cfg.kname],
                          self.dram_t[cfg.vname], self.dram_t[cfg.oname])
        for h in range(NH):
            r0 = h * hd
            em.dma("q_sp", qh.ap(0, hd, 0, T), qd[r0:r0 + hd, :],
                   reads=dram_keys(cfg.qname), writes=qh.keys())
            em.dma("q_sp", kh.ap(0, hd, 0, T), kd[r0:r0 + hd, :],
                   reads=dram_keys(cfg.kname), writes=kh.keys())
            em.dma("q_act", vh.ap(0, 128, 0, (T // 128) * hd).rearrange("p (rc n) -> p rc n", n=hd),
                   vd.rearrange("(rc p) n -> p rc n", p=128)[:, :, r0:r0 + hd],
                   reads=dram_keys(cfg.vname), writes=vh.keys())
            for qg in range(nqg):
                kwg = (qg + 1) * 512
                for j in range(4):
                    q0 = qg * 512 + j * 128
                    kw = q0 + 128
                    nkb = (kw + 511) // 512
                    sbank = f"ps{3 + (j % 2)}"
                    sb = int(sbank[2])
                    Sb, Pb, Pnb = S[j % 2], P[j % 2], Pn[j % 2]
                    for kb in range(nkb):
                        n = min(512, kw - kb * 512)
                        em.op("pe", lambda e, q0=q0, kb=kb, n=n, sb=sb: e.matmul(
                            self.ps[sb][:, 0:n], qh.ap(0, hd, q0, q0 + 128),
                            kh.ap(0, hd, kb * 512, kb * 512 + n), start=True,
                            stop=(cfg.rope is None)),
                            reads=qh.keys(q0, q0 + 128) + kh.keys(kb * 512, kb * 512 + n),
                            writes=[sbank], inc=(cfg.rope is None))
                        if rope is not None:
                            qr, kr, rdim = rope
                            em.op("pe", lambda e, q0=q0, kb=kb, n=n, sb=sb, qr=qr, kr=kr, rdim=rdim: e.matmul(
                                self.ps[sb][:, 0:n], qr.ap(0, rdim, q0, q0 + 128),
                                kr.ap(0, rdim, kb * 512, kb * 512 + n), start=False, stop=True),
                                reads=qr.keys(q0, q0 + 128) + kr.keys(kb * 512, kb * 512 + n),
                                writes=[sbank], inc=True)
                        plain_n = (n - 128) if kb == nkb - 1 else n
                        eng = "dve" if kb % 2 == 0 else "act"
                        if plain_n > 0:
                            if eng == "dve":
                                em.op(eng, lambda e, kb=kb, pn=plain_n, sb=sb, Sb=Sb: e.tensor_copy(
                                    Sb.ap(0, 128, kb * 512, kb * 512 + pn), self.ps[sb][:, 0:pn]),
                                    reads=[sbank], writes=Sb.keys(kb * 512, kb * 512 + plain_n))
                            else:
                                em.op(eng, lambda e, kb=kb, pn=plain_n, sb=sb, Sb=Sb: e.activation(
                                    out=Sb.ap(0, 128, kb * 512, kb * 512 + pn), in_=self.ps[sb][:, 0:pn], func=AF.Identity),
                                    reads=[sbank], writes=Sb.keys(kb * 512, kb * 512 + plain_n))
                        if kb == nkb - 1:
                            em.op("dve", lambda e, q0=q0, kb=kb, n=n, sb=sb, Sb=Sb: e.tensor_tensor(
                                out=Sb.ap(0, 128, q0, q0 + 128), in0=self.ps[sb][:, n - 128:n],
                                in1=self.const("c_causal"), op=OP.add),
                                reads=[sbank, ("const", "c_causal")], writes=Sb.keys(q0, q0 + 128))
                    em.op("dve", lambda e, kw=kw, j=j, Sb=Sb: e.reduce_max(
                        mcol[j].ap(0, 128, 0, 1), Sb.ap(0, 128, 0, kw), axis=AX),
                        reads=Sb.keys(0, kw), writes=mcol[j].keys())
                    em.op("dve", lambda e, j=j: e.tensor_scalar_mul(
                        bcol[j].ap(0, 128, 0, 1), mcol[j].ap(0, 128, 0, 1), -scale),
                        reads=mcol[j].keys(), writes=bcol[j].keys())
                    em.op("act", lambda e, kw=kw, j=j, Sb=Sb, Pb=Pb: e.activation(
                        out=Pb.ap(0, 128, 0, kw), in_=Sb.ap(0, 128, 0, kw), func=AF.Exp,
                        bias=bcol[j].ap(0, 128, 0, 1), scale=sc_ap,
                        accum_out=scol[j].ap(0, 128, 0, 1)),
                        reads=Sb.keys(0, kw) + bcol[j].keys() + sc_keys,
                        writes=Pb.keys(0, kw) + scol[j].keys())
                    em.op("dve", lambda e, j=j: e.reciprocal(
                        rcol[j].ap(0, 128, 0, 1), scol[j].ap(0, 128, 0, 1)),
                        reads=scol[j].keys(), writes=rcol[j].keys())
                    em.op("dve", lambda e, kw=kw, j=j, Pb=Pb, Pnb=Pnb: e.tensor_scalar_mul(
                        Pnb.ap(0, 128, 0, kw), Pb.ap(0, 128, 0, kw), rcol[j].ap(0, 128, 0, 1)),
                        reads=Pb.keys(0, kw) + rcol[j].keys(), writes=Pnb.keys(0, kw))
                    if kw < kwg:
                        em.op("dve", lambda e, kw=kw, kwg=kwg, Pnb=Pnb: e.memset(
                            Pnb.ap(0, 128, kw, kwg), 0.0),
                            reads=[], writes=Pnb.keys(kw, kwg))
                    for rc in range(kwg // 128):
                        ptb = f"pt{rc % 2}"
                        slot = (rc // 2) % 4 if False else 0
                        em.op("pe", lambda e, rc=rc, j=j, ptb=ptb, Pnb=Pnb: e.transpose(
                            self.pt[int(ptb[2])][:, j * 128:(j + 1) * 128],
                            Pnb.ap(0, 128, rc * 128, rc * 128 + 128), self.const("c_ident_bf")),
                            reads=Pnb.keys(rc * 128, rc * 128 + 128) + [("const", "c_ident_bf")],
                            writes=[ptb], inc=True)
                        em.op("dve", lambda e, rc=rc, j=j, ptb=ptb: e.tensor_copy(
                            PT.ap(0, 128, rc * 512 + j * 128, rc * 512 + (j + 1) * 128),
                            self.pt[int(ptb[2])][:, j * 128:(j + 1) * 128]),
                            reads=[ptb], writes=PT.keys(rc * 512 + j * 128, rc * 512 + (j + 1) * 128))
                for rc in range(kwg // 128):
                    em.op("pe", lambda e, rc=rc, last=(rc == kwg // 128 - 1): e.matmul(
                        self.ps[5][0:hd, :], vh.ap(0, 128, rc * hd, (rc + 1) * hd),
                        PT.ap(0, 128, rc * 512, (rc + 1) * 512),
                        start=(rc == 0), stop=last),
                        reads=vh.keys(rc * hd, (rc + 1) * hd) + PT.keys(rc * 512, (rc + 1) * 512),
                        writes=["ps5"], inc=(rc == kwg // 128 - 1))
                eng = "dve" if qg % 2 == 0 else "act"
                if eng == "dve":
                    em.op(eng, lambda e, qg=qg: e.tensor_copy(
                        oh.ap(0, hd, qg * 512, (qg + 1) * 512), self.ps[5][0:hd, :]),
                        reads=["ps5"], writes=oh.keys(qg * 512, (qg + 1) * 512))
                else:
                    em.op(eng, lambda e, qg=qg: e.activation(
                        out=oh.ap(0, hd, qg * 512, (qg + 1) * 512), in_=self.ps[5][0:hd, :], func=AF.Identity),
                        reads=["ps5"], writes=oh.keys(qg * 512, (qg + 1) * 512))
            em.dma("q_sp", od[r0:r0 + hd, :], oh.ap(0, hd, 0, T),
                   reads=oh.keys(), writes=dram_keys(cfg.oname))

    def zero_dram(self, name, rows, cols):
        em = self.em
        cw = min(2048, cols)
        z = self.af32.alloc(cw)
        em.op("dve", lambda e: e.memset(z.ap(0, 128, 0, cw), 0.0),
              reads=[], writes=z.keys())
        d = self.dram_t[name]
        for rc in range(rows // 128):
            for cb in range(cols // cw):
                em.dma("q_sp", d[rc * 128:(rc + 1) * 128, cb * cw:(cb + 1) * cw],
                       z.ap(0, 128, 0, cw), reads=z.keys(),
                       writes=[("D", name), ("D", name + "_zero")])

    def router_top2(self, pre):
        em = self.em
        cT = self.af32.alloc(T)
        lt = self.af32.alloc(16 * 2)
        probs = self.af32.alloc(16)
        work = [self.asm.alloc(1) for _ in range(4)]
        mx, sm, rc_, m2 = work
        msk = self.af32.alloc(16)
        ld = self.dram_t["l3_logits"]
        for tc in range(T // 128):
            sl = (tc % 2) * 16
            em.dma("q_sp", lt.ap(0, 128, sl, sl + 8), ld[tc * 128:(tc + 1) * 128, :],
                   reads=dram_keys("l3_logits"), writes=lt.keys(sl, sl + 8))
            em.op("dve", lambda e, sl=sl: e.reduce_max(mx.ap(0, 128, 0, 1), lt.ap(0, 128, sl, sl + 8), axis=AX),
                  reads=lt.keys(sl, sl + 8), writes=mx.keys())
            em.op("dve", lambda e: e.tensor_scalar_mul(m2.ap(0, 128, 0, 1), mx.ap(0, 128, 0, 1), -1.0),
                  reads=mx.keys(), writes=m2.keys())
            em.op("act", lambda e, sl=sl: e.activation(out=probs.ap(0, 128, 0, 8), in_=lt.ap(0, 128, sl, sl + 8),
                                                       func=AF.Exp, bias=m2.ap(0, 128, 0, 1),
                                                       accum_out=sm.ap(0, 128, 0, 1)),
                  reads=lt.keys(sl, sl + 8) + m2.keys(), writes=probs.keys() + sm.keys())
            em.op("dve", lambda e: e.reciprocal(rc_.ap(0, 128, 0, 1), sm.ap(0, 128, 0, 1)),
                  reads=sm.keys(), writes=rc_.keys())
            em.op("dve", lambda e, sl=sl: e.tensor_scalar_mul(lt.ap(0, 128, sl + 8, sl + 16), probs.ap(0, 128, 0, 8), rc_.ap(0, 128, 0, 1)),
                  reads=probs.keys() + rc_.keys(), writes=lt.keys(sl + 8, sl + 16))
            pr = lambda: lt.ap(0, 128, sl + 8, sl + 16)
            prk = lt.keys(sl + 8, sl + 16)
            em.op("dve", lambda e, sl=sl: e.reduce_max(mx.ap(0, 128, 0, 1), lt.ap(0, 128, sl + 8, sl + 16), axis=AX),
                  reads=prk, writes=mx.keys())
            em.op("dve", lambda e, sl=sl: e.tensor_scalar(out=msk.ap(0, 128, 0, 8), in0=lt.ap(0, 128, sl + 8, sl + 16),
                                                          scalar1=mx.ap(0, 128, 0, 1), scalar2=None, op0=OP.is_ge),
                  reads=prk + mx.keys(), writes=msk.keys())
            em.op("dve", lambda e, sl=sl: e.scalar_tensor_tensor(out=probs.ap(0, 128, 0, 8), in0=msk.ap(0, 128, 0, 8),
                                                                 scalar=-1e9, in1=lt.ap(0, 128, sl + 8, sl + 16),
                                                                 op0=OP.mult, op1=OP.add),
                  reads=msk.keys() + prk, writes=probs.keys())
            em.op("dve", lambda e: e.reduce_max(m2.ap(0, 128, 0, 1), probs.ap(0, 128, 0, 8), axis=AX),
                  reads=probs.keys(), writes=m2.keys())
            em.op("dve", lambda e, sl=sl: e.tensor_scalar(out=msk.ap(0, 128, 0, 8), in0=lt.ap(0, 128, sl + 8, sl + 16),
                                                          scalar1=m2.ap(0, 128, 0, 1), scalar2=None, op0=OP.is_ge),
                  reads=prk + m2.keys(), writes=msk.keys())
            em.op("dve", lambda e, sl=sl: e.tensor_tensor(out=probs.ap(0, 128, 0, 8), in0=lt.ap(0, 128, sl + 8, sl + 16),
                                                          in1=msk.ap(0, 128, 0, 8), op=OP.mult),
                  reads=prk + msk.keys(), writes=probs.keys())
            em.op("pe", lambda e: e.transpose(self.ps[2][0:8, 0:128], probs.ap(0, 128, 0, 8), self.const("c_ident_f")),
                  reads=probs.keys() + [("const", "c_ident_f")], writes=["ps2"])
            em.op("dve", lambda e, tc=tc: e.tensor_copy(cT.ap(0, 8, tc * 128, (tc + 1) * 128), self.ps[2][0:8, 0:128]),
                  reads=["ps2"], writes=cT.keys(tc * 128, (tc + 1) * 128))
        cd = self.dram("l3_cT", [8, T], F32)
        em.dma("q_sp", cd[:, :], cT.ap(0, 8, 0, T), reads=cT.keys(),
               writes=[("D", "l3_cT")])
        return "l3_cT"

    def moe_swiglu(self, xn_name, out_name, wg_name, wu_name, wd_name, I,
                   cT=None, crow=None, cconst=None, first=False, wrow0=0):
        em = self.em
        ffn = I // 128
        for tq in range(T // 1024):
            _obf, _o32, _osm = self.abf.off, self.af32.off, self.asm.off
            t0 = tq * 1024
            xn_q = self.load_fm(xn_name, D, c0=t0, c1=t0 + 1024)
            h = self.abf.alloc(ffn * 1024)
            _opass = self.abf.off
            wgsl = [self.abf.alloc(D) for _ in range(2)]
            wusl = [self.abf.alloc(D) for _ in range(2)]
            g_sb = self.abf.alloc(1024)
            wg_d, wu_d = self.dram_t[wg_name], self.dram_t[wu_name]
            for fm in range(ffn):
                gs, us = wgsl[fm % 2], wusl[fm % 2]
                for sl, wd_ in ((gs, wg_d), (us, wu_d)):
                    em.dma("q_pool", sl.ap().rearrange("p (c m) -> p c m", c=8),
                           wd_[wrow0:wrow0 + D, fm * 128:(fm + 1) * 128].rearrange("(c p) m -> p c m", p=128),
                           reads=[("D", wg_name)], writes=sl.keys())
                for xk in range(8):
                    for u in range(2):
                        em.op("pe", lambda e, xk=xk, u=u, gs=gs: e.matmul(
                            self.ps[u][:, :], gs.ap(0, 128, xk * 128, (xk + 1) * 128),
                            xn_q.ap(0, 128, xk * 1024 + u * 512, xk * 1024 + u * 512 + 512),
                            start=(xk == 0), stop=(xk == 7)),
                            reads=gs.keys(xk * 128, (xk + 1) * 128) + xn_q.keys(xk * 1024 + u * 512, xk * 1024 + u * 512 + 512),
                            writes=[f"ps{u}"], inc=(xk == 7 and u == 1))
                for xk in range(8):
                    for u in range(2):
                        em.op("pe", lambda e, xk=xk, u=u, us=us: e.matmul(
                            self.ps[2 + u][:, :], us.ap(0, 128, xk * 128, (xk + 1) * 128),
                            xn_q.ap(0, 128, xk * 1024 + u * 512, xk * 1024 + u * 512 + 512),
                            start=(xk == 0), stop=(xk == 7)),
                            reads=us.keys(xk * 128, (xk + 1) * 128) + xn_q.keys(xk * 1024 + u * 512, xk * 1024 + u * 512 + 512),
                            writes=[f"ps{2 + u}"], inc=(xk == 7 and u == 1))
                for u in range(2):
                    em.op("act", lambda e, u=u: e.activation(out=g_sb.ap(0, 128, u * 512, u * 512 + 512),
                                                             in_=self.ps[u][:, :], func=AF.Silu),
                          reads=[f"ps{u}"], writes=g_sb.keys(u * 512, u * 512 + 512))
                    em.op("dve", lambda e, u=u, fm=fm: e.scalar_tensor_tensor(
                        out=h.ap(0, 128, fm * 1024 + u * 512, fm * 1024 + u * 512 + 512),
                        in0=self.ps[2 + u][:, :], scalar=1.0, in1=g_sb.ap(0, 128, u * 512, u * 512 + 512),
                        op0=OP.mult, op1=OP.mult),
                        reads=[f"ps{2 + u}"] + g_sb.keys(u * 512, u * 512 + 512),
                        writes=h.keys(fm * 1024 + u * 512, fm * 1024 + u * 512 + 512))
            self.abf.off = _opass
            wdsl = [self.abf.alloc(I) for _ in range(2)]
            stage = [self.af32.alloc(1024) for _ in range(2)]
            bc_sb = self.af32.alloc(1024)
            wd_d = self.dram_t[wd_name]
            if cT is not None:
                crow_r = self.af32.alloc(1024)
                cd = self.dram_t[cT]
                em.dma("q_sp", crow_r.ap(0, 1, 0, 1024), cd[crow:crow + 1, t0:t0 + 1024],
                       reads=[("D", cT)], writes=crow_r.keys())
                for u in range(2):
                    em.op("pe", lambda e, u=u, crow_r=crow_r: e.matmul(
                        self.ps[4][:, :], self.const("c_ones_row_f"),
                        crow_r.ap(0, 1, u * 512, u * 512 + 512),
                        start=True, stop=True),
                        reads=crow_r.keys(u * 512, u * 512 + 512) + [("const", "c_ones_row_f")],
                        writes=["ps4"])
                    em.op("dve", lambda e, u=u: e.tensor_copy(bc_sb.ap(0, 128, u * 512, u * 512 + 512), self.ps[4][:, :]),
                          reads=["ps4"], writes=bc_sb.keys(u * 512, u * 512 + 512))
            out_d = self.dram_t[out_name]
            for m in range(8):
                sl = wdsl[m % 2]
                em.dma("q_pool", sl.ap().rearrange("p (c m) -> p c m", c=ffn),
                       wd_d[:, m * 128:(m + 1) * 128].rearrange("(c p) m -> p c m", p=128),
                       reads=[("D", wd_name)], writes=sl.keys())
                for fk in range(ffn):
                    for u in range(2):
                        em.op("pe", lambda e, fk=fk, u=u, sl=sl: e.matmul(
                            self.ps[u][:, :], sl.ap(0, 128, fk * 128, (fk + 1) * 128),
                            h.ap(0, 128, fk * 1024 + u * 512, fk * 1024 + u * 512 + 512),
                            start=(fk == 0), stop=(fk == ffn - 1)),
                            reads=sl.keys(fk * 128, (fk + 1) * 128) + h.keys(fk * 1024 + u * 512, fk * 1024 + u * 512 + 512),
                            writes=[f"ps{u}"], inc=(fk == ffn - 1 and u == 1))
                st = stage[m % 2]
                for u in range(2):
                    if cT is not None:
                        em.op("dve", lambda e, u=u, st=st: e.tensor_tensor(
                            out=st.ap(0, 128, u * 512, u * 512 + 512), in0=self.ps[u][:, :],
                            in1=bc_sb.ap(0, 128, u * 512, u * 512 + 512), op=OP.mult),
                            reads=[f"ps{u}"] + bc_sb.keys(u * 512, u * 512 + 512),
                            writes=st.keys(u * 512, u * 512 + 512))
                    else:
                        em.op("dve", lambda e, u=u, st=st: e.tensor_scalar_mul(
                            st.ap(0, 128, u * 512, u * 512 + 512), self.ps[u][:, :], float(cconst)),
                            reads=[f"ps{u}"], writes=st.keys(u * 512, u * 512 + 512))
                em.dma("q_pool", out_d[m * 128:(m + 1) * 128, t0:t0 + 1024], st.ap(0, 128, 0, 1024),
                       reads=st.keys() + [("D", out_name + "_zero")],
                       writes=[("D", out_name)], no_waw=True, accum_op=OP.add)
            self.abf.off, self.af32.off, self.asm.off = _obf, _o32, _osm


class AttnCfg:
    def __init__(self, NH, hd, scale, qname, kname, vname, oname,
                 rope=None, vw=None):
        self.NH, self.hd, self.scale = NH, hd, scale
        self.qname, self.kname, self.vname, self.oname = qname, kname, vname, oname
        self.rope = rope
        self.vw = vw or NH * hd



def build_block0(serialize=False, xkey="x_in", okey="x_out"):
    p = Prog(serialize)
    p.inp(xkey, [D, T], F32)
    for nm, shape in (("l0wq", [D, D]), ("l0bq", [D]), ("l0wk", [D, D]), ("l0bk", [D]),
                      ("l0wv", [D, D]), ("l0bv", [D]), ("l0wo", [D, D]), ("l0bo", [D]),
                      ("l0U1", [D, 256]), ("l0V1", [256, 4096]), ("l0b1", [4096]),
                      ("l0U2", [4096, 256]), ("l0V2", [256, D]), ("l0b2", [D])):
        p.inp(nm, shape, F32)
    p.dram("xn", [D, T], BF16)
    p.outp(okey, [D, T], F32)
    p.ln(xkey, "xn")
    p.attn_l0("l0")
    p.ffn_l0("l0")
    p.combine(xkey, "s_blk", "f_blk", okey)
    nc = p.em.finish()
    INPUT_NAMES["block0"] = set(p.input_names)
    return nc, dict(p.em.host_inputs)


def prep_block0_inputs(inp):
    perm = np.array([dh * 8 + h for h in range(8) for dh in range(128)])
    w = np.asarray(inp["L0_attn_w_qkv"])
    b = np.asarray(inp["L0_attn_b_qkv"])
    cc = np.ascontiguousarray
    return {
        "l0wq": cc(w[0:1024][perm].T), "l0bq": cc(b[0:1024][perm]),
        "l0wk": cc(w[1024:2048][perm].T), "l0bk": cc(b[1024:2048][perm]),
        "l0wv": cc(w[2048:3072][perm].T), "l0bv": cc(b[2048:3072][perm]),
        "l0wo": cc(np.asarray(inp["L0_attn_w_o"]).T), "l0bo": np.asarray(inp["L0_attn_b_o"]),
        "l0U1": np.asarray(inp["L0_ffn_U1"]), "l0V1": np.asarray(inp["L0_ffn_V1"]),
        "l0b1": np.asarray(inp["L0_ffn_b1"]), "l0U2": np.asarray(inp["L0_ffn_U2"]),
        "l0V2": np.asarray(inp["L0_ffn_V2"]), "l0b2": np.asarray(inp["L0_ffn_b2"]),
    }



_LAST_PROG = [None]


def build_block3(serialize=False, xkey="x_in", okey="x_out"):
    p = Prog(serialize)
    _LAST_PROG[0] = p
    p.inp(xkey, [D, T], F32)
    for nm, shape in (("l3Wdq", [D, 256]), ("l3Wuq", [256, D]), ("l3Wdkv", [D, 256]),
                      ("l3Wuk", [256, D]), ("l3Wuv", [256, D]), ("l3Wo", [D, D]),
                      ("l3Wr", [D, 8]), ("l3Sg", [D, 2048]), ("l3Su", [D, 2048]),
                      ("l3Sd", [2048, D])):
        p.inp(nm, shape, F32)
    for e in range(8):
        p.inp(f"l3Wg{e}", [D, 4096], F32)
        p.inp(f"l3Wu{e}", [D, 4096], F32)
        p.inp(f"l3Wd{e}", [4096, D], F32)
    p.dram("xn", [D, T], BF16)
    p.dram("s_blk", [D, T], F32)
    p.dram("f_blk", [D, T], F32)
    for nm, shape in (("l3_cq", [256, T]), ("l3_ckv", [256, T]), ("l3_q", [D, T]),
                      ("l3_k", [D, T]), ("l3_ot", [D, T])):
        p.dram(nm, shape, BF16)
    p.dram("l3_v", [T, D], BF16)
    p.dram("l3_logits", [T, 8], F32)
    p.outp(okey, [D, T], F32)

    p.ln(xkey, "xn")
    p.newphase()
    xn_r = p.load_fm("xn", D)
    p.proj_fm("l3_cq", xn_r, "l3Wdq", K=D, M=256)
    p.proj_fm("l3_ckv", xn_r, "l3Wdkv", K=D, M=256)
    p.proj_tm("l3_logits", xn_r, "l3Wr", K=D, N=8, out_dtype=F32)
    p.newphase()
    cq_r = p.load_fm("l3_cq", 256)
    p.proj_fm("l3_q", cq_r, "l3Wuq", K=256, M=D)
    ckv_r = p.load_fm("l3_ckv", 256)
    p.proj_fm("l3_k", ckv_r, "l3Wuk", K=256, M=D)
    p.proj_tm("l3_v", ckv_r, "l3Wuv", K=256, N=D)
    p.attn_causal(AttnCfg(16, 64, 1.0 / 8.0, "l3_q", "l3_k", "l3_v", "l3_ot"))
    p.newphase()
    ot_r = p.load_fm("l3_ot", D)
    p.proj_fm("s_blk", ot_r, "l3Wo", K=D, M=D, out_dtype=F32)
    p.newphase()
    p.zero_dram("f_blk", D, T)
    cT = p.router_top2("l3")
    for e in range(8):
        p.moe_swiglu("xn", "f_blk", f"l3Wg{e}", f"l3Wu{e}", f"l3Wd{e}", I=4096,
                     cT=cT, crow=e)
    p.moe_swiglu("xn", "f_blk", "l3Sg", "l3Su", "l3Sd", I=2048, cconst=0.5)
    p.combine(xkey, "s_blk", "f_blk", okey)
    nc = p.em.finish()
    INPUT_NAMES["block3"] = set(p.input_names)
    return nc, dict(p.em.host_inputs)


def prep_block3_inputs(inp):
    cc = np.ascontiguousarray
    d = {
        "l3Wdq": np.asarray(inp["L3_attn_W_dq"]), "l3Wuq": np.asarray(inp["L3_attn_W_uq"]),
        "l3Wdkv": np.asarray(inp["L3_attn_W_dkv"]), "l3Wuk": np.asarray(inp["L3_attn_W_uk"]),
        "l3Wuv": np.asarray(inp["L3_attn_W_uv"]), "l3Wo": np.asarray(inp["L3_attn_W_o"]),
        "l3Wr": np.asarray(inp["L3_ffn_Wr"]), "l3Sg": np.asarray(inp["L3_ffn_Sg"]),
        "l3Su": np.asarray(inp["L3_ffn_Su"]), "l3Sd": np.asarray(inp["L3_ffn_Sd"]),
    }
    for e in range(8):
        d[f"l3Wg{e}"] = np.asarray(inp["L3_ffn_Wg"])[e]
        d[f"l3Wu{e}"] = np.asarray(inp["L3_ffn_Wu"])[e]
        d[f"l3Wd{e}"] = np.asarray(inp["L3_ffn_Wd"])[e]
    return d


def build_block2(serialize=False, xkey="x_in", okey="x_out"):
    p = Prog(serialize)
    p.inp(xkey, [D, T], F32)
    for nm, shape in (("l2WdqT", [D, 512]), ("l2WuqT", [512, D]), ("l2WdkvT", [D, 512]),
                      ("l2Wuk", [D, 512]), ("l2Wuv", [D, 512]), ("l2WoT", [D, D])):
        p.inp(nm, shape, F32)
    for nm in ("l2WgT", "l2WuT", "l2WdT"):
        p.inp(nm, [D, 3072] if nm != "l2WdT" else [3072, D], I32)
    p.dram("xn", [D, T], BF16)
    p.dram("s_blk", [D, T], F32)
    p.dram("f_blk", [D, T], F32)
    for nm, shape in (("l2_B", [D, D]), ("l2_keff", [D, 512]), ("l2_veff", [512, D]),
                      ("l2_qc", [512, T]), ("l2_q", [D, T]), ("l2_ckv", [512, T]),
                      ("l2_g", [3072, T]), ("l2_u", [3072, T]), ("l2_h", [3072, T])):
        p.dram(nm, shape, BF16)
    p.dram("l2_ckv_tok", [T, 512], BF16)
    p.outp(okey, [D, T], F32)

    p.inp("l2Wdq_raw", [512, D], F32)
    p.ln(xkey, "xn")
    p.newphase()
    wdq_raw = p.load_fm("l2Wdq_raw", 512, cols=1024)
    p.proj_fm("l2_B", wdq_raw, "l2WuqT", K=512, M=D, Tc=1024)
    wuk_r = p.load_fm("l2Wuk", D, cols=512)
    p.proj_fm("l2_keff", wuk_r, "l2_B", K=D, M=D, Tc=512)
    woT_r = p.load_fm("l2WoT", D, cols=1024)
    p.proj_fm("l2_veff", woT_r, "l2Wuv", K=D, M=512, Tc=1024)
    p.newphase()
    xn_r = p.load_fm("xn", D)
    p.proj_fm("l2_qc", xn_r, "l2WdqT", K=D, M=512)
    p.proj_fm("l2_ckv", xn_r, "l2WdkvT", K=D, M=512)
    p.proj_tm("l2_ckv_tok", xn_r, "l2WdkvT", K=D, N=512)
    p.newphase()
    qc_r = p.load_fm("l2_qc", 512)
    p.proj_fm("l2_q", qc_r, "l2WuqT", K=512, M=D)
    p.attn_l2()
    s127 = 1.0 / 127.0
    p.newphase()
    xn_r = p.load_fm("xn", D)
    p.proj_fm("l2_g", xn_r, "l2WgT", K=D, M=3072, actf=AF.Silu, scale=s127)
    p.proj_fm("l2_u", xn_r, "l2WuT", K=D, M=3072)
    p.newphase()
    p.ew_mul("l2_h", "l2_g", "l2_u", 3072)
    for kg in range(3):
        p.newphase()
        h_r = p._load_fm_rows("l2_h", 3072, kg * 1024, 1024)
        p.proj_fm("f_blk", h_r, "l2WdT", K=1024, M=D, out_dtype=F32,
                  scale=s127 * s127, w_src_rows=kg * 1024, accum=(kg > 0))
    p.combine(xkey, "s_blk", "f_blk", okey)
    nc = p.em.finish()
    INPUT_NAMES["block2"] = set(p.input_names)
    return nc, dict(p.em.host_inputs)


def prep_block2_inputs(inp):
    cc = np.ascontiguousarray
    return {
        "l2WdqT": cc(np.asarray(inp["L2_attn_W_dq"]).T),
        "l2WuqT": cc(np.asarray(inp["L2_attn_W_uq"]).T),
        "l2WdkvT": cc(np.asarray(inp["L2_attn_W_dkv"]).T),
        "l2Wuk": np.asarray(inp["L2_attn_W_uk"]),
        "l2Wuv": np.asarray(inp["L2_attn_W_uv"]),
        "l2WoT": cc(np.asarray(inp["L2_attn_W_o"]).T),
        "l2Wdq_raw": np.asarray(inp["L2_attn_W_dq"]),
        "l2WgT": cc(np.asarray(inp["L2_ffn_w_gate"]).T),
        "l2WuT": cc(np.asarray(inp["L2_ffn_w_up"]).T),
        "l2WdT": cc(np.asarray(inp["L2_ffn_w_down"]).T),
    }

_BUILD_CACHE = {}
INPUT_NAMES = {}


def _build_cached(key, fn):
    if key not in _BUILD_CACHE:
        _BUILD_CACHE[key] = fn()
    return _BUILD_CACHE[key]


def _rope_apply(p, raw_name, rot_name):
    em = p.em
    _obf = p.abf.off
    xa = p.abf.alloc(T)
    xb = p.abf.alloc(T)
    t1 = p.abf.alloc(T)
    t2 = p.abf.alloc(T)
    ro = p.abf.alloc(T)
    z16 = p.abf.alloc(T)
    rd = p.dram_t[raw_name]
    ot = p.dram(rot_name, [48, T], BF16)
    em.dma("q_sp", xa.ap(0, 16, 0, T), rd[0:16, :], reads=dram_keys(raw_name), writes=xa.keys())
    em.dma("q_sp", xb.ap(0, 16, 0, T), rd[32:48, :], reads=dram_keys(raw_name), writes=xb.keys())
    cos = p.consts["c_cos"][0:16, :]
    sin = p.consts["c_sin"][0:16, :]
    ck = [("const", "c_cos")]
    sk = [("const", "c_sin")]
    em.op("dve", lambda e: e.tensor_tensor(out=t1.ap(0, 16, 0, T), in0=xa.ap(0, 16, 0, T), in1=cos, op=OP.mult),
          reads=xa.keys() + ck, writes=t1.keys())
    em.op("dve", lambda e: e.tensor_tensor(out=t2.ap(0, 16, 0, T), in0=xb.ap(0, 16, 0, T), in1=sin, op=OP.mult),
          reads=xb.keys() + sk, writes=t2.keys())
    em.op("dve", lambda e: e.tensor_tensor(out=ro.ap(0, 16, 0, T), in0=t1.ap(0, 16, 0, T), in1=t2.ap(0, 16, 0, T), op=OP.subtract),
          reads=t1.keys() + t2.keys(), writes=ro.keys())
    em.dma("q_sp", ot[0:16, :], ro.ap(0, 16, 0, T), reads=ro.keys(), writes=dram_keys(rot_name))
    em.op("dve", lambda e: e.tensor_tensor(out=t1.ap(0, 16, 0, T), in0=xb.ap(0, 16, 0, T), in1=cos, op=OP.mult),
          reads=xb.keys() + ck, writes=t1.keys())
    em.op("dve", lambda e: e.tensor_tensor(out=t2.ap(0, 16, 0, T), in0=xa.ap(0, 16, 0, T), in1=sin, op=OP.mult),
          reads=xa.keys() + sk, writes=t2.keys())
    em.op("dve", lambda e: e.tensor_tensor(out=ro.ap(0, 16, 0, T), in0=t1.ap(0, 16, 0, T), in1=t2.ap(0, 16, 0, T), op=OP.add),
          reads=t1.keys() + t2.keys(), writes=ro.keys())
    em.dma("q_sp", ot[32:48, :], ro.ap(0, 16, 0, T), reads=ro.keys(), writes=dram_keys(rot_name))
    em.op("dve", lambda e: e.memset(z16.ap(0, 16, 0, T), 0.0), reads=[], writes=z16.keys())
    em.dma("q_sp", ot[16:32, :], z16.ap(0, 16, 0, T), reads=z16.keys(), writes=dram_keys(rot_name))
    p.abf.off = _obf


def build_block1a(serialize=False, xkey="x_in"):
    p = Prog(serialize)
    p.inp(xkey, [D, T], F32)
    for nm, shape in (("l1Wkv", [D, 128]), ("l1Wkup", [128, D]), ("l1Wvup", [128, D]),
                      ("l1Wq", [D, D]), ("l1Wqrope", [D, 128]), ("l1Wkrope", [128, 128]),
                      ("l1Wo", [D, D])):
        p.inp(nm, shape, F32)
    p.outp("xn", [D, T], BF16)
    p.outp("s_blk", [D, T], F32)
    for nm, shape in (("l1_ckv", [128, T]), ("l1_k", [D, T]), ("l1_q", [D, T]),
                      ("l1_qr_raw", [128, T]), ("l1_kr_raw", [128, T]), ("l1_ot", [D, T])):
        p.dram(nm, shape, BF16)
    p.dram("l1_v", [T, D], BF16)
    p.ln(xkey, "xn")
    p.newphase()
    xn_r = p.load_fm("xn", D)
    p.proj_fm("l1_ckv", xn_r, "l1Wkv", K=D, M=128)
    p.proj_fm("l1_q", xn_r, "l1Wq", K=D, M=D)
    p.proj_fm("l1_qr_raw", xn_r, "l1Wqrope", K=D, M=128)
    p.newphase()
    ckv_r = p.load_fm("l1_ckv", 128)
    p.proj_fm("l1_k", ckv_r, "l1Wkup", K=128, M=D)
    p.proj_tm("l1_v", ckv_r, "l1Wvup", K=128, N=D)
    p.proj_fm("l1_kr_raw", ckv_r, "l1Wkrope", K=128, M=128)
    p.newphase()
    _rope_apply(p, "l1_qr_raw", "l1_qrot")
    _rope_apply(p, "l1_kr_raw", "l1_krot")
    p.attn_causal(AttnCfg(16, 64, 1.0 / math.sqrt(96.0), "l1_q", "l1_k", "l1_v",
                          "l1_ot", rope=("l1_qrot", "l1_krot", 48)))
    p.newphase()
    ot_r = p.load_fm("l1_ot", D)
    p.proj_fm("s_blk", ot_r, "l1Wo", K=D, M=D, out_dtype=F32)
    nc = p.em.finish()
    INPUT_NAMES["block1a"] = set(p.input_names)
    return nc, dict(p.em.host_inputs)


def build_block1b(serialize=False):
    p = Prog(serialize)
    p.inp("xn_all", [D, T], BF16)
    p.inp("l1fWqkvT", [D, 3072], F32)
    p.inp("l1fbqkv", [3072], F32)
    p.inp("l1fWvT", [D, D], F32)
    p.inp("l1fbv", [D], F32)
    p.inp("l1fWoT", [D, D], F32)
    p.inp("l1fbo", [D], F32)
    p.inp("l1fg1", [D], F32)
    p.inp("l1fbe1", [D], F32)
    p.inp("l1fg2", [D], F32)
    p.inp("l1fbe2", [D], F32)
    p.inp("l1fWgT", [D, 8], F32)
    p.inp("l1fb1", [8, 4096], F32)
    p.inp("l1fb2", [8, D], F32)
    for e in range(8):
        p.inp(f"l1fW1{e}", [D, 4096], F32)
        p.inp(f"l1fW2{e}", [4096, D], F32)
    for nm, shape in (("l1f_qkv", [3072, T]), ("l1f_at", [D, T]), ("y_bf", [D, T])):
        p.dram(nm, shape, BF16)
    p.dram("l1f_vtok", [T, D], BF16)
    p.dram("l1f_attn", [D, T], F32)
    p.dram("v1", [D, T], F32)
    p.dram("l1f_moe", [D, T], F32)
    p.dram("v2", [D, T], F32)
    p.dram("l1f_logits", [T, 8], F32)
    p.outp("z32", [D, T], F32)
    p.dram("z_bf", [D, T], BF16)

    p.newphase()
    xn_r = p.load_fm("xn_all", D)
    p.proj_fm("l1f_qkv", xn_r, "l1fWqkvT", K=D, M=3072, bias="l1fbqkv")
    p.proj_tm("l1f_vtok", xn_r, "l1fWvT", K=D, N=D, bias="l1fbv")
    p.attn_inner("l1f_qkv", "l1f_vtok", "l1f_at")
    p.newphase()
    at_r = p.load_fm("l1f_at", D)
    p.proj_fm("l1f_attn", at_r, "l1fWoT", K=D, M=D, bias="l1fbo", out_dtype=F32)
    p.newphase()
    p.ew_add("v1", "xn_all", "l1f_attn", D)
    p.ln("v1", "y_bf", gain="l1fg1", bias="l1fbe1")
    p.newphase()
    y_r = p.load_fm("y_bf", D)
    p.proj_tm("l1f_logits", y_r, "l1fWgT", K=D, N=8, out_dtype=F32)
    p.newphase()
    p.zero_dram("l1f_moe", D, T)
    p.router_top1("l1f_logits", "l1f_cT")
    for e in range(8):
        p.moe_relu("y_bf", "l1f_moe", f"l1fW1{e}", "l1fb1", f"l1fW2{e}", "l1f_cT", e)
    p.moe_b2("l1f_moe", "l1fb2", "l1f_cT")
    p.newphase()
    p.ew_add("v2", "y_bf", "l1f_moe", D)
    p.ln("v2", "z_bf", gain="l1fg2", bias="l1fbe2", dst32_name="z32")
    nc = p.em.finish()
    INPUT_NAMES["block1b"] = set(p.input_names)
    return nc, dict(p.em.host_inputs)


def build_block1c(serialize=False):
    p = Prog(serialize)
    p.inp("x_in", [D, T], F32)
    p.inp("s_in", [D, T], F32)
    p.inp("f_in", [D, T], F32)
    p.outp("x_out", [D, T], F32)
    p.combine("x_in", "s_in", "f_in", "x_out")
    nc = p.em.finish()
    INPUT_NAMES["block1c"] = set(p.input_names)
    return nc, dict(p.em.host_inputs)


def prep_block1_inputs(inp):
    cc = np.ascontiguousarray
    wqr = np.asarray(inp["L1_attn_W_q_rope"])
    wkr = np.asarray(inp["L1_attn_W_k_rope"])
    qr_p = np.zeros((D, 128), np.float32)
    qr_p[:, 0:16] = wqr[:, 0:16]
    qr_p[:, 32:48] = wqr[:, 16:32]
    kr_p = np.zeros((128, 128), np.float32)
    kr_p[:, 0:16] = wkr[:, 0:16]
    kr_p[:, 32:48] = wkr[:, 16:32]
    wqkv = np.asarray(inp["L1_ffn_Wqkv"])
    d = {
        "l1Wkv": np.asarray(inp["L1_attn_W_kv_down"]),
        "l1Wkup": np.asarray(inp["L1_attn_W_k_up"]),
        "l1Wvup": np.asarray(inp["L1_attn_W_v_up"]),
        "l1Wq": np.asarray(inp["L1_attn_W_q"]),
        "l1Wqrope": qr_p, "l1Wkrope": kr_p,
        "l1Wo": np.asarray(inp["L1_attn_W_o"]),
        "l1fWqkvT": cc(wqkv.T),
        "l1fbqkv": np.asarray(inp["L1_ffn_bqkv"]),
        "l1fWvT": cc(wqkv[2048:3072].T),
        "l1fbv": np.asarray(inp["L1_ffn_bqkv"])[2048:3072],
        "l1fWoT": cc(np.asarray(inp["L1_ffn_Wo"]).T),
        "l1fbo": np.asarray(inp["L1_ffn_bo"]),
        "l1fg1": np.asarray(inp["L1_ffn_g1"]), "l1fbe1": np.asarray(inp["L1_ffn_be1"]),
        "l1fg2": np.asarray(inp["L1_ffn_g2"]), "l1fbe2": np.asarray(inp["L1_ffn_be2"]),
        "l1fWgT": cc(np.asarray(inp["L1_ffn_Wg"]).T),
        "l1fb1": np.asarray(inp["L1_ffn_b1"]),
        "l1fb2": np.asarray(inp["L1_ffn_b2"]),
    }
    for e in range(8):
        d[f"l1fW1{e}"] = np.asarray(inp["L1_ffn_W1"])[e]
        d[f"l1fW2{e}"] = np.asarray(inp["L1_ffn_W2"])[e]
    return d


def run_launch(build_key, build_fn, shared, per_core, out_names):
    nc, consts = _build_cached(build_key, build_fn)
    names = INPUT_NAMES[build_key]
    in_maps = [{k: v for k, v in {**consts, **shared, **pc}.items() if k in names or k in consts}
               for pc in per_core]
    res = run_bass_kernel_spmd(nc, in_maps, core_ids=list(range(NCORES)))
    return [{k: r[k] for k in out_names} for r in res.results]


def kernel(**inputs):
    cc = np.ascontiguousarray
    x = np.asarray(inputs["L0_attn_x"])
    xT = [cc(x[b].T) for b in range(NCORES)]
    sh0 = prep_block0_inputs(inputs)
    sh1 = prep_block1_inputs(inputs)
    sh2 = prep_block2_inputs(inputs)
    sh3 = prep_block3_inputs(inputs)

    r0 = run_launch("block0", build_block0, sh0,
                    [{"x_in": xT[b]} for b in range(NCORES)], ["x_out"])
    x1 = [r0[b]["x_out"] for b in range(NCORES)]

    r1a = run_launch("block1a", build_block1a, sh1,
                     [{"x_in": x1[b]} for b in range(NCORES)], ["xn", "s_blk"])
    xn_st = np.stack([r1a[b]["xn"] for b in range(NCORES)])
    xn_all = [cc(xn_st[:, :, 512 * c:512 * (c + 1)].transpose(1, 2, 0).reshape(D, T))
              for c in range(NCORES)]
    r1b = run_launch("block1b", build_block1b, sh1,
                     [{"xn_all": xn_all[c]} for c in range(NCORES)], ["z32"])
    z = []
    for b in range(NCORES):
        zb = np.empty((D, T), np.float32)
        for c in range(NCORES):
            zb[:, 512 * c:512 * (c + 1)] = r1b[c]["z32"][:, b::8]
        z.append(cc(zb))
    r1c = run_launch("block1c", build_block1c, {},
                     [{"x_in": x1[b], "s_in": r1a[b]["s_blk"], "f_in": z[b]}
                      for b in range(NCORES)], ["x_out"])
    x2 = [r1c[b]["x_out"] for b in range(NCORES)]

    r2 = run_launch("block2", build_block2, sh2,
                    [{"x_in": x2[b]} for b in range(NCORES)], ["x_out"])
    x3 = [r2[b]["x_out"] for b in range(NCORES)]

    r3 = run_launch("block3", build_block3, sh3,
                    [{"x_in": x3[b]} for b in range(NCORES)], ["x_out"])
    out = np.stack([r3[b]["x_out"].T for b in range(NCORES)], axis=0)
    return np.ascontiguousarray(out.astype(np.float32))
